# version 09
# baseline (speedup 1.0000x reference)
import numpy as np
import ml_dtypes
from contextlib import ExitStack
import concourse.bass as bass
import concourse.mybir as mybir
from concourse.bass_utils import run_bass_kernel_spmd

F32 = mybir.dt.float32
BF16 = mybir.dt.bfloat16
I32 = mybir.dt.int32
AF = mybir.ActivationFunctionType
ALU = mybir.AluOpType
AX = mybir.AxisListType

B, S, D, H, HD = 16, 2048, 1024, 16, 64
E, CAP = 32, 1280
T = B * S
NCORE = 8
NT = T // NCORE
NE = E // NCORE
NSLOT = NE * CAP
TBL = 41088
DUMP = 40960.0
SENT = 1.0e9
NCH = D // 128
P1T = 512
NP1 = NT // P1T
P4T = 256
NP4 = NT // P4T
EPS = 1e-5
DEBUG = False
STAGE = 99
NP1_RUN = 8
NP4_RUN = 16


def G(cond, reg):
    def deco(fn):
        if cond:
            reg(fn)
    return deco


class Ctr:
    def __init__(self, h):
        self.h = h
        self.n = 0

    def inc(self, instr, k=1):
        instr.then_inc(self.h, k)
        self.n += k
        return self.n


def build_nc():
    nc = bass.Bass("TRN2", target_bir_lowering=False, debug=False)

    x_own = nc.dram_tensor("x_own", [NT, D], F32, kind="ExternalInput").ap()
    x_all = nc.dram_tensor("x_all", [T, D], F32, kind="ExternalInput").ap()
    wq_t = nc.dram_tensor("wq_t", [D, D], F32, kind="ExternalInput").ap()
    wv_t = nc.dram_tensor("wv_t", [D, D], F32, kind="ExternalInput").ap()
    wg_t = nc.dram_tensor("wg_t", [D, D], F32, kind="ExternalInput").ap()
    wo_t = nc.dram_tensor("wo_t", [D, D], F32, kind="ExternalInput").ap()
    wr_h = nc.dram_tensor("wr_h", [D, E], F32, kind="ExternalInput").ap()
    weg = nc.dram_tensor("weg", [NE, D, D], F32, kind="ExternalInput").ap()
    weu = nc.dram_tensor("weu", [NE, D, D], F32, kind="ExternalInput").ap()
    wed = nc.dram_tensor("wed", [NE, D, D], F32, kind="ExternalInput").ap()
    gam_h = nc.dram_tensor("gam_h", [128, NCH], F32, kind="ExternalInput").ap()
    bet_h = nc.dram_tensor("bet_h", [128, NCH], F32, kind="ExternalInput").ap()
    lam_h = nc.dram_tensor("lam_h", [128, NCH], F32, kind="ExternalInput").ap()
    lamb_h = nc.dram_tensor("lamb_h", [128, NCH * P4T], F32, kind="ExternalInput").ap()
    ids_h = nc.dram_tensor("ids_h", [128, 32], F32, kind="ExternalInput").ap()
    widx_h = nc.dram_tensor("widx_h", [40, 1], I32, kind="ExternalInput").ap()
    perm_h = nc.dram_tensor("perm_h", [128, 128], F32, kind="ExternalInput").ap()
    iotae_h = nc.dram_tensor("iotae_h", [128, E], F32, kind="ExternalInput").ap()
    c32_h = nc.dram_tensor("c32_h", [128, E], F32, kind="ExternalInput").ap()
    ident_h = nc.dram_tensor("ident_h", [128, 128], F32, kind="ExternalInput").ap()
    ltu_h = nc.dram_tensor("ltu_h", [128, 128], BF16, kind="ExternalInput").ap()
    onc_h = nc.dram_tensor("onc_h", [128, 1], BF16, kind="ExternalInput").ap()
    onr_h = nc.dram_tensor("onr_h", [1, 128], F32, kind="ExternalInput").ap()

    y_out = nc.dram_tensor("y_out", [NT, D], F32, kind="ExternalOutput").ap()
    if DEBUG:
        dbg_t1 = nc.dram_tensor("dbg_t1", [128, 256], F32, kind="ExternalOutput").ap()
        dbg_slot = nc.dram_tensor("dbg_slot", [128, 256], F32, kind="ExternalOutput").ap()
        dbg_pos = nc.dram_tensor("dbg_pos", [128, 256], F32, kind="ExternalOutput").ap()
        dbg_idl = nc.dram_tensor("dbg_idl", [TBL, 1], F32, kind="ExternalOutput").ap()
        dbg_win = nc.dram_tensor("dbg_win", [128, 40], F32, kind="ExternalOutput").ap()
        dbg_xnc = nc.dram_tensor("dbg_xnc", [NSLOT, D], F32, kind="ExternalOutput").ap()
        dbg_gate = nc.dram_tensor("dbg_gate", [128, 32], F32, kind="ExternalOutput").ap()
        dbg_gm = nc.dram_tensor("dbg_gm", [128, 32], F32, kind="ExternalOutput").ap()
        dbg_xnfm = nc.dram_tensor("dbg_xnfm", [128, NT], F32, kind="ExternalOutput").ap()
        dbg_xnsc = nc.dram_tensor("dbg_xnsc", [NT, D], F32, kind="ExternalOutput").ap()
        dbg_sm = nc.dram_tensor("dbg_sm2", [NT, 1], F32, kind="ExternalOutput").ap()

        dbg_ps0 = nc.dram_tensor("dbg_ps0", [32, P1T], F32, kind="ExternalOutput").ap()
        dbg_q = nc.dram_tensor("dbg_q", [128, NCH * P4T], F32, kind="ExternalOutput").ap()
        dbg_v = nc.dram_tensor("dbg_v", [128, NCH * P4T], F32, kind="ExternalOutput").ap()
        dbg_st = nc.dram_tensor("dbg_st", [128, NCH * P4T], F32, kind="ExternalOutput").ap()
        dbg_yg = nc.dram_tensor("dbg_yg", [128, NCH * P4T], F32, kind="ExternalOutput").ap()
        dbg_s = nc.dram_tensor("dbg_s", [NT, D], F32, kind="ExternalOutput").ap()
        dbg_y3 = nc.dram_tensor("dbg_y3", [NSLOT, D], F32, kind="ExternalOutput").ap()

    xn_sc = nc.dram_tensor("xn_sc", [NT, D], BF16)
    s_sc = nc.dram_tensor("s_sc", [NT, D], BF16)
    ag1_in = nc.dram_tensor("ag1_in", [NT, 1], BF16)
    ag1_out = nc.dram_tensor("ag1_out", [T, 1], BF16, addr_space="Shared")
    ptab = nc.dram_tensor("ptab", [TBL, 1], F32)
    ag2_out = nc.dram_tensor("ag2_out", [NCORE * TBL, 1], F32, addr_space="Shared")
    idl_hb = nc.dram_tensor("idl_hb", [TBL, 1], F32)
    smine_h = nc.dram_tensor("smine_h", [NT, 1], F32)
    win_hb = nc.dram_tensor("win_hb", [NSLOT, 1], F32)
    xnc_hb = nc.dram_tensor("xnc_hb", [NSLOT, D], BF16)
    ag3_in = nc.dram_tensor("ag3_in", [NSLOT, D], BF16)
    ag3_out = nc.dram_tensor("ag3_out", [E * CAP, D], BF16, addr_space="Shared")

    with ExitStack() as ctx:
        ent = ctx.enter_context
        wr_hi = ent(nc.sbuf_tensor("wr_hi", [128, NCH * E], BF16))
        gam_c = ent(nc.sbuf_tensor("gam_c", [128, NCH], F32))
        bet_c = ent(nc.sbuf_tensor("bet_c", [128, NCH], F32))
        lam_c = ent(nc.sbuf_tensor("lam_c", [128, NCH], F32))
        lam_bt = ent(nc.sbuf_tensor("lam_bt", [128, NCH * P4T], F32))
        ids_f = ent(nc.sbuf_tensor("ids_f", [128, 32], F32))
        widx = ent(nc.sbuf_tensor("widx", [40, 1], I32))
        permk = ent(nc.sbuf_tensor("permk", [128, 128], F32))
        iotae = ent(nc.sbuf_tensor("iotae", [128, E], F32))
        c32 = ent(nc.sbuf_tensor("c32", [128, E], F32))
        ident = ent(nc.sbuf_tensor("ident", [128, 128], F32))
        ltu = ent(nc.sbuf_tensor("ltu", [128, 128], BF16))
        onc = ent(nc.sbuf_tensor("onc", [128, 1], BF16))
        onr = ent(nc.sbuf_tensor("onr", [1, 128], F32))
        gate_cols = ent(nc.sbuf_tensor("gate_cols", [128, 32], F32))
        top_cols = ent(nc.sbuf_tensor("top_cols", [128, 32], F32))
        sent_t = ent(nc.sbuf_tensor("sent_t", [128, 321], F32))
        t1_all = ent(nc.sbuf_tensor("t1_all", [128, 256], BF16))
        wint = ent(nc.sbuf_tensor("wint", [40, 128], F32))
        winr_i = ent(nc.sbuf_tensor("winr_i", [128, 40], I32))
        slotT_i = ent(nc.sbuf_tensor("slotT_i", [128, 32], I32))
        gmask = ent(nc.sbuf_tensor("gmask", [128, 32], F32))
        carry = ent(nc.sbuf_tensor("carry", [128, NCH], F32))
        xgb = ent(nc.sbuf_tensor("xgb", [128, 2 * D], F32))
        xnc_b = ent(nc.sbuf_tensor("xnc_b", [128, 2 * D], BF16))
        st3 = ent(nc.sbuf_tensor("st3", [128, 16], F32))
        jkv = ent(nc.sbuf_tensor("jkv", [128, 8], F32))
        jks = ent(nc.sbuf_tensor("jks", [128, 8], F32))
        jkb = ent(nc.sbuf_tensor("jkb", [128, 1], BF16))
        slm = ent(nc.sbuf_tensor("slm", [128, 32], F32))
        slm_i = ent(nc.sbuf_tensor("slm_i", [128, 32], I32))
        smT = ent(nc.sbuf_tensor("smT", [32, 128], F32))
        t1t = ent(nc.sbuf_tensor("t1t", [32, 128], BF16))
        winr_f = ent(nc.sbuf_tensor("winr_f", [128, 40], F32))
        pblk = ent(nc.sbuf_tensor("pblk", [128, 2 * 321], F32))

        def vfin(v, ctr, k=1):
            v.tensor_scalar(jkv[:, 0:1], jkv[:, 0:1], 1.0, None, ALU.mult)
            ctr.inc(v.drain(), k)

        def sfin(sc, ctr, k=1, op=None):
            if op == "silu":
                sc.activation(jkb[:], jks[:, 0:1], AF.Silu)
            elif op == "square":
                sc.activation(jkb[:], jks[:, 0:1], AF.Square)
            else:
                sc.mul(jks[:, 0:1], jks[:, 0:1], 1.0)
            ctr.inc(sc.drain(), k)

        names = ("sW sX sXN sXNF sT sV sSc sG sCC sS2 sG3 sL3 sST3 sWE sTD "
                 "sY3 sS4 sP5 sG5 sOUT").split()
        cs = {k: Ctr(ent(nc.semaphore(k))) for k in names}
        sW, sX, sXN, sXNF, sT_, sV, sSc, sG, sCC, sS2, sG3, sL3, sST3, sWE, sTD, \
            sY3, sS4, sP5, sG5, sOUT = (cs[k] for k in names)

        block = ent(nc.Block())

        @block.sync
        def _(sy):
            for dst, src in ((gam_c, gam_h), (bet_c, bet_h), (lam_c, lam_h)):
                sW.inc(sy.dma_start(out=dst[:], in_=src), 16)
            sW.inc(sy.dma_start(out=lam_bt[:], in_=lamb_h), 16)
            for dst, src in ((ids_f, ids_h), (widx, widx_h), (permk, perm_h),
                             (iotae, iotae_h), (c32, c32_h), (ident, ident_h),
                             (ltu, ltu_h), (onc, onc_h), (onr, onr_h)):
                sW.inc(sy.dma_start(out=dst[:], in_=src), 16)
        w_consts = sW.n

        @block.scalar
        def _(sc):
            sc.memzero(jks[:])

        @block.vector
        def _(v):
            v.memset(jkv[:], 0.0)
            v.memset(sent_t[:], SENT)
            vfin(v, sV)
        v_p0 = sV.n

        @block.sync
        def _(sy):
            sy.wait_ge(sV.h, v_p0)
            sS2.inc(sy.dma_start(out=ptab.ap().rearrange("(p c) one -> p (c one)", p=128),
                                 in_=sent_t[:]), 16)

        s2_sent = sS2.n

        with ExitStack() as ctxA:
            entA = ctxA.enter_context
            wq_sb = entA(nc.sbuf_tensor("wq_sb", [128, NCH * D], BF16))
            wv_sb = entA(nc.sbuf_tensor("wv_sb", [128, NCH * D], BF16))
            wgt_sb = entA(nc.sbuf_tensor("wgt_sb", [128, NCH * D], BF16))
            wo_sb = entA(nc.sbuf_tensor("wo_sb", [128, NCH * D], BF16))
            xn_fm = entA(nc.sbuf_tensor("xn_fm", [128, NCH * NT], BF16))

            def wch(wsb, d):
                return wsb[:].rearrange("p (c d) -> p c d", c=NCH)[:, d]

            def xch(d, lo, hi):
                return xn_fm[:, d * NT + lo: d * NT + hi]

            @block.gpsimd
            def _(g):
                sW.inc(g.dma_start(out=wr_hi[:].rearrange("p (c e) -> p c e", c=NCH),
                                   in_=wr_h.rearrange("(c p) e -> p c e", p=128)), 16)
                for w_h, w_sb in ((wq_t, wq_sb), (wv_t, wv_sb), (wg_t, wgt_sb), (wo_t, wo_sb)):
                    sW.inc(g.dma_start(out=w_sb[:].rearrange("p (c d) -> p c d", c=NCH),
                                       in_=w_h.rearrange("(c p) d -> p c d", p=128)), 16)
            w_all = sW.n

            with ExitStack() as ctx1:
                ent1 = ctx1.enter_context
                xr = ent1(nc.sbuf_tensor("xr", [128, 4 * D], F32))
                xn_tok = ent1(nc.sbuf_tensor("xn_tok", [128, 4 * D], BF16))
                lsb = ent1(nc.sbuf_tensor("lsb", [32, P1T], F32))
                lgt = ent1(nc.sbuf_tensor("lgt", [128, 4 * E], F32))
                lge = ent1(nc.sbuf_tensor("lge", [128, 4 * E], F32))
                eqb = ent1(nc.sbuf_tensor("eqb", [128, 4 * E], F32))
                rc = ent1(nc.sbuf_tensor("rc", [128, 24], F32))
                slotf = ent1(nc.sbuf_tensor("slotf", [128, 32], F32))
                tot = ent1(nc.sbuf_tensor("tot", [128, E], F32))
                tot_bf = ent1(nc.sbuf_tensor("tot_bf", [128, E], BF16))
                rp_sb = ent1(nc.sbuf_tensor("rp_sb", [128, E], F32))
                ones_t = ent1(nc.sbuf_tensor("ones_t", [128, 256], F32))
                rank = ent1(nc.sbuf_tensor("rank", [128, 256], F32))
                ohb = ent1(nc.sbuf_tensor("ohb", [128, 256], F32))
                scn = ent1(nc.sbuf_tensor("scn", [128, 256], F32))
                pos_t = ent1(nc.sbuf_tensor("pos_t", [128, 256], F32))
                slot_t = ent1(nc.sbuf_tensor("slot_t", [128, 256], F32))
                ps_r = ent1(nc.psum_tensor("ps_r", [32, P1T], F32))
                ps_lt = ent1(nc.psum_tensor("ps_lt", [128, 128], F32))
                ps_t1 = ent1(nc.psum_tensor("ps_t1", [32, 128], F32))
                ps_rp = ent1(nc.psum_tensor("ps_rp", [128, E], F32))
                ps_pm = ent1(nc.psum_tensor("ps_pm", [128, 256], F32))

                stat_done, xn_done, lsb_done, neg_done, gates_done = [], [], [], [], []
                for i in range(min(NP1, NP1_RUN)):
                    xri = xr[:].rearrange("p (s d) -> p s d", s=4)

                    @block.sync
                    def _(sy, i=i, xri=xri):
                        if i >= 1:
                            sy.wait_ge(sV.h, xn_done[i - 1])
                        sX.inc(sy.dma_start(
                            out=xri,
                            in_=x_own.rearrange("(i s p) d -> p i s d", s=4, p=128)[:, i]), 16)

                    @block.scalar
                    def _(sc, i=i, xri=xri):
                        sc.wait_ge(sX.h, 16 * (i + 1))
                        for s_ in range(4):
                            sc.activation(xn_tok[:, s_ * D:(s_ + 1) * D], xri[:, s_],
                                          AF.Square, accum_out=rc[:, s_:s_ + 1])
                        sfin(sc, sSc, op="square")
                    sq_done = sSc.n

                    @block.vector
                    def _(v, i=i, xri=xri, sq_done=sq_done):
                        v.wait_ge(sX.h, 16 * (i + 1))
                        v.wait_ge(sSc.h, sq_done)
                        for s_ in range(4):
                            v.reduce_sum(rc[:, 4 + s_:5 + s_], xri[:, s_], axis=AX.X)
                        for s_ in range(4):
                            v.tensor_scalar(rc[:, 4 + s_:5 + s_], rc[:, 4 + s_:5 + s_],
                                            1.0 / D, None, ALU.mult)
                        for s_ in range(4):
                            v.tensor_scalar(rc[:, s_:s_ + 1], rc[:, s_:s_ + 1],
                                            1.0 / D, None, ALU.mult)
                        for s_ in range(4):
                            v.tensor_scalar(rc[:, 20 + s_:21 + s_], rc[:, 4 + s_:5 + s_],
                                            rc[:, 4 + s_:5 + s_], None, ALU.mult)
                        for s_ in range(4):
                            v.tensor_scalar(rc[:, s_:s_ + 1], rc[:, s_:s_ + 1],
                                            rc[:, 20 + s_:21 + s_], None, ALU.subtract)
                        for s_ in range(4):
                            v.tensor_scalar(rc[:, s_:s_ + 1], rc[:, s_:s_ + 1],
                                            EPS, None, ALU.add)
                        for s_ in range(4):
                            v.reciprocal(rc[:, s_:s_ + 1], rc[:, s_:s_ + 1])
                        vfin(v, sV)
                    stat_done.append(sV.n)

                    @block.scalar
                    def _(sc, i=i, xri=xri):
                        sc.wait_ge(sV.h, stat_done[i])
                        for s_ in range(4):
                            sc.sqrt(rc[:, 8 + s_:9 + s_], rc[:, s_:s_ + 1])
                        for s_ in range(4):
                            sc.mul(rc[:, 12 + s_:13 + s_], rc[:, 4 + s_:5 + s_], -1.0)
                        for s_ in range(4):
                            sc.mul(rc[:, 12 + s_:13 + s_], rc[:, 12 + s_:13 + s_],
                                   rc[:, 8 + s_:9 + s_])
                        sfin(sc, sSc)
                    ist_done = sSc.n

                    @block.vector
                    def _(v, i=i, xri=xri, ist_done=ist_done):
                        v.wait_ge(sSc.h, ist_done)
                        if i >= 1:
                            v.wait_ge(sXN.h, 16 * i)
                        for s_ in range(4):
                            v.tensor_scalar(xn_tok[:, s_ * D:(s_ + 1) * D], xri[:, s_],
                                            rc[:, 8 + s_:9 + s_], rc[:, 12 + s_:13 + s_],
                                            ALU.mult, ALU.add)
                        vfin(v, sV)
                    xn_done.append(sV.n)

                    @block.sync
                    def _(sy, i=i):
                        sy.wait_ge(sV.h, xn_done[i])
                        sXN.inc(sy.dma_start(
                            out=xn_sc.ap().rearrange("(i s p) d -> p i s d", s=4, p=128)[:, i],
                            in_=xn_tok[:].rearrange("p (s d) -> p s d", s=4)), 16)
                        sy.wait_ge(sXN.h, sXN.n)
                        for d in range(NCH):
                            sXNF.inc(sy.dma_start(
                                out=xch(d, i * P1T, (i + 1) * P1T),
                                in_=xn_sc.ap()[i * P1T:(i + 1) * P1T, d * 128:(d + 1) * 128],
                                transpose=True), 16)
                    xnf_done = sXNF.n

                    @block.tensor
                    def _(t, i=i, xnf_done=xnf_done):
                        t.wait_ge(sW.h, w_consts)
                        t.wait_ge(sV.h, v_p0)
                        t.wait_ge(sXNF.h, xnf_done)
                        if i >= 1:
                            t.wait_ge(sV.h, lsb_done[i - 1])
                        for d in range(NCH):
                            ins = t.matmul(ps_r[:], wr_hi[:].rearrange("p (c e) -> p c e", c=NCH)[:, d],
                                           xch(d, i * P1T, (i + 1) * P1T),
                                           start=(d == 0), stop=(d == NCH - 1))
                        sT_.inc(ins)
                    rt_done = sT_.n

                    @block.vector
                    def _(v, i=i, rt_done=rt_done):
                        v.wait_ge(sT_.h, rt_done)
                        v.tensor_copy(lsb[:], ps_r[:])
                        if DEBUG and i == 0:
                            v.tensor_copy(wrtmp[:, 0:256] if False else lsb[:], ps_r[:])
                        vfin(v, sV)
                    lsb_done.append(sV.n)

                    if DEBUG and i == 0:
                        @block.sync
                        def _(sy):
                            sy.wait_ge(sV.h, lsb_done[0])
                            sOUT.inc(sy.dma_start(out=dbg_ps0, in_=lsb[:]), 16)

                    @block.tensor
                    def _(t, i=i):
                        t.wait_ge(sV.h, lsb_done[i])
                        if i >= 1:
                            t.wait_ge(sV.h, neg_done[i - 1])
                        for s_ in range(4):
                            ins = t.transpose(ps_lt[:, s_ * E:(s_ + 1) * E],
                                              lsb[:, s_ * 128:(s_ + 1) * 128],
                                              ident[0:32, 0:32])
                        sT_.inc(ins)
                    lt_done = sT_.n

                    @block.vector
                    def _(v, i=i, lt_done=lt_done):
                        v.wait_ge(sT_.h, lt_done)
                        if i >= 1:
                            v.wait_ge(sSc.h, exp_done[i - 1])
                        v.tensor_copy(lgt[:], ps_lt[:])
                        for s_ in range(4):
                            v.reduce_max(rc[:, 16 + s_:17 + s_], lgt[:, s_ * E:(s_ + 1) * E], axis=AX.X)
                        for s_ in range(4):
                            v.tensor_scalar(eqb[:, s_ * E:(s_ + 1) * E], lgt[:, s_ * E:(s_ + 1) * E],
                                            rc[:, 16 + s_:17 + s_], None, ALU.is_equal)
                        for s_ in range(4):
                            v.tensor_tensor(lge[:, s_ * E:(s_ + 1) * E],
                                            eqb[:, s_ * E:(s_ + 1) * E], c32[:], ALU.mult)
                        for s_ in range(4):
                            v.tensor_scalar(lge[:, s_ * E:(s_ + 1) * E],
                                            lge[:, s_ * E:(s_ + 1) * E], float(E), None, ALU.add)
                        for s_ in range(4):
                            v.tensor_reduce(top_cols[:, 4 * i + s_:4 * i + s_ + 1],
                                            lge[:, s_ * E:(s_ + 1) * E], axis=AX.X, op=ALU.min)
                        for s_ in range(4):
                            v.tensor_scalar(rc[:, 16 + s_:17 + s_], rc[:, 16 + s_:17 + s_],
                                            -1.0, None, ALU.mult)
                        vfin(v, sV)
                    neg_done.append(sV.n)

                    @block.scalar
                    def _(sc, i=i):
                        sc.wait_ge(sV.h, neg_done[i])
                        for s_ in range(4):
                            sc.activation(lge[:, s_ * E:(s_ + 1) * E],
                                          lgt[:, s_ * E:(s_ + 1) * E],
                                          AF.Exp, bias=rc[:, 16 + s_:17 + s_], scale=1.0)
                        sfin(sc, sSc)
                    if i == 0:
                        exp_done = []
                    exp_done.append(sSc.n)

                    @block.vector
                    def _(v, i=i):
                        v.wait_ge(sSc.h, exp_done[i])
                        for s_ in range(4):
                            v.reduce_sum(rc[:, 20 + s_:21 + s_], lge[:, s_ * E:(s_ + 1) * E], axis=AX.X)
                        for s_ in range(4):
                            v.reciprocal(gate_cols[:, 4 * i + s_:4 * i + s_ + 1],
                                         rc[:, 20 + s_:21 + s_])
                        vfin(v, sV)
                    gates_done.append(sV.n)

                @G(bool(gates_done), block.tensor)
                def _(t):
                    t.wait_ge(sV.h, gates_done[-1])
                    sT_.inc(t.transpose(ps_t1[:], top_cols[:], ident[:]))
                t1_done = sT_.n

                @block.vector
                def _(v):
                    v.wait_ge(sT_.h, t1_done)
                    v.tensor_copy(t1t[:], ps_t1[:])
                    vfin(v, sV)
                t1t_done = sV.n

                @block.gpsimd
                def _(g):
                    g.wait_ge(sV.h, t1t_done)
                    sG.inc(g.dma_start(out=ag1_in.ap().rearrange("(j p) one -> j (p one)", p=128),
                                       in_=t1t[:]), 16)
                    g.wait_ge(sG.h, sG.n)
                    sCC.inc(g.collective_compute(
                        "AllGather", ALU.bypass, replica_groups=[list(range(NCORE))],
                        ins=[ag1_in.ap()], outs=[ag1_out.ap()]))
                    g.wait_ge(sCC.h, sCC.n)
                    sG.inc(g.dma_start(out=t1_all[:],
                                       in_=ag1_out.rearrange("(p c) one -> p (c one)", p=128)), 16)
                t1all_done = sG.n

                @G(STAGE >= 12, block.vector)
                def _(v):
                    v.wait_ge(sG.h, t1all_done)
                    v.memset(ones_t[:], 1.0)
                    v.memset(rank[:], 0.0)
                    for e in range(E):
                        v.tensor_scalar(ohb[:], t1_all[:], float(e), None, ALU.is_equal)
                        v.tensor_tensor_scan(scn[:], ones_t[:], ohb[:], 0.0, ALU.mult, ALU.add)
                        v.tensor_tensor(ohb[:], scn[:], ohb[:], ALU.mult)
                        v.tensor_add(rank[:], rank[:], ohb[:])
                        v.tensor_copy(tot[:, e:e + 1], scn[:, 255:256])
                    v.tensor_scalar(jkv[:, 1:2], jkv[:, 1:2], 1.0, None, ALU.mult)
                    v.tensor_scalar(jkv[:, 2:3], jkv[:, 2:3], 1.0, None, ALU.mult)
                    v.tensor_copy(tot_bf[:], tot[:])
                    vfin(v, sV)
                tot_done = sV.n

                @G(STAGE >= 12, block.tensor)
                def _(t):
                    t.wait_ge(sV.h, tot_done)
                    sT_.inc(t.matmul(ps_rp[:], ltu[:], tot_bf[:], start=True, stop=True))
                rp_done = sT_.n

                @G(STAGE >= 12, block.vector)
                def _(v):
                    v.wait_ge(sT_.h, rp_done)
                    v.tensor_copy(rp_sb[:], ps_rp[:])
                    v.tensor_scalar(jkv[:, 1:2], jkv[:, 1:2], 1.0, None, ALU.mult)
                    for e in range(E):
                        v.tensor_scalar(ohb[:], t1_all[:], float(e), None, ALU.is_equal)
                        v.tensor_scalar(ohb[:], ohb[:], rp_sb[:, e:e + 1], None, ALU.mult)
                        v.tensor_add(rank[:], rank[:], ohb[:])
                    v.tensor_scalar(pos_t[:], rank[:], -1.0, None, ALU.add)
                    v.tensor_scalar(slot_t[:], t1_all[:], float(CAP), None, ALU.mult)
                    v.tensor_add(slot_t[:], slot_t[:], pos_t[:])
                    v.tensor_scalar(ohb[:], pos_t[:], float(CAP - 1), None, ALU.min)
                    v.tensor_tensor(ohb[:], ohb[:], pos_t[:], ALU.is_equal)
                    v.tensor_scalar(slot_t[:], slot_t[:], -DUMP, None, ALU.add)
                    v.tensor_tensor(slot_t[:], slot_t[:], ohb[:], ALU.mult)
                    v.tensor_scalar(slot_t[:], slot_t[:], DUMP, None, ALU.add)
                    vfin(v, sV)
                slot_done = sV.n

                @G(STAGE >= 13, block.tensor)
                def _(t):
                    t.wait_ge(sV.h, slot_done)
                    sT_.inc(t.matmul(ps_pm[:], permk[:], slot_t[:], start=True, stop=True))
                pm_done = sT_.n

                @G(STAGE >= 13, block.vector)
                def _(v):
                    v.wait_ge(sT_.h, pm_done)
                    v.tensor_copy(lsb[0:16, 0:256], ps_pm[0:16, :])
                    vfin(v, sV)
                sm_done = sV.n

                @G(STAGE >= 13, block.sync)
                def _(sy):
                    sy.wait_ge(sV.h, sm_done)
                    sS2.inc(sy.dma_start(out=smine_h.ap().rearrange("(r c) one -> r (c one)", r=16),
                                         in_=lsb[0:16, 0:256]), 16)
                    sy.wait_ge(sS2.h, sS2.n)
                    sS2.inc(sy.dma_start(out=slm[:],
                                         in_=smine_h.ap().rearrange("(p c) one -> p (c one)", p=128)), 16)
                    sS2.inc(sy.dma_start(out=smT[:],
                                         in_=smine_h.ap().rearrange("(j p) one -> j (p one)", p=128)), 16)
                sm_ld = sS2.n

                @G(STAGE >= 13, block.vector)
                def _(v):
                    v.wait_ge(sS2.h, sm_ld)
                    v.tensor_scalar(slm[:], slm[:], float(TBL - 1), 0.0, ALU.min, ALU.max)
                    v.tensor_scalar(jkv[:, 1:2], jkv[:, 1:2], 1.0, None, ALU.mult)
                    v.tensor_copy(slm_i[:], slm[:])
                    vfin(v, sV)
                slmi_done = sV.n

                @G(STAGE >= 13, block.tensor)
                def _(t):
                    t.wait_ge(sS2.h, sm_ld)
                    sT_.inc(t.transpose(ps_pm[:, 0:32], smT[:], ident[0:32, 0:32]))
                smt_done = sT_.n

                @G(STAGE >= 13, block.vector)
                def _(v):
                    v.wait_ge(sT_.h, smt_done)
                    v.tensor_copy(slotf[:], ps_pm[:, 0:32])
                    v.tensor_scalar(jkv[:, 1:2], jkv[:, 1:2], 1.0, None, ALU.mult)
                    v.tensor_scalar(gmask[:], slotf[:], DUMP, None, ALU.is_equal)
                    v.tensor_scalar(slotf[:], slotf[:], DUMP - 1.0, 0.0, ALU.min, ALU.max)
                    v.tensor_scalar(gmask[:], gmask[:], -1.0, 1.0, ALU.mult, ALU.add)
                    v.tensor_copy(slotT_i[:], slotf[:])
                    v.tensor_tensor(gmask[:], gmask[:], gate_cols[:], ALU.mult)
                    vfin(v, sV)
                p5idx_done = sV.n

                @G(STAGE >= 14, block.gpsimd)
                def _(g):
                    g.wait_ge(sV.h, slmi_done)
                    g.wait_ge(sS2.h, s2_sent)
                    for cc in range(32):
                        sG.inc(g.indirect_dma_start(
                            out=ptab.ap(),
                            out_offset=bass.IndirectOffsetOnAxis(ap=slm_i[:, cc:cc + 1], axis=0),
                            in_=ids_f[:, cc:cc + 1], in_offset=None), 16)
                    g.wait_ge(sG.h, sG.n)
                    sCC.inc(g.collective_compute(
                        "AllGather", ALU.bypass, replica_groups=[list(range(NCORE))],
                        ins=[ptab.ap()], outs=[ag2_out.ap()]))
                ag2_done = sCC.n

                pb_marks = []
                mn_marks = []
                for r in range(NCORE):
                    @G(STAGE >= 14, block.sync)
                    def _(sy, r=r):
                        if r == 0:
                            sy.wait_ge(sCC.h, ag2_done)
                        if r >= 2:
                            sy.wait_ge(sV.h, mn_marks[r - 2])
                        sS2.inc(sy.dma_start(
                            out=pblk[:, (r % 2) * 321:(r % 2 + 1) * 321],
                            in_=ag2_out.rearrange("(r p c) one -> r p (c one)", p=128, c=321)[r]), 16)
                    pb_marks.append(sS2.n)

                    @G(STAGE >= 14, block.vector)
                    def _(v, r=r):
                        v.wait_ge(sS2.h, pb_marks[r])
                        if r == 0:
                            v.tensor_scalar(sent_t[:], pblk[:, 0:321], 1.0, None, ALU.mult)
                        else:
                            v.tensor_tensor(sent_t[:], sent_t[:],
                                            pblk[:, (r % 2) * 321:(r % 2 + 1) * 321], ALU.min)
                        vfin(v, sV)
                    mn_marks.append(sV.n)
                idl_done = sV.n

                @G(STAGE >= 14, block.sync)
                def _(sy):
                    sy.wait_ge(sV.h, idl_done)
                    sS2.inc(sy.dma_start(out=idl_hb.ap().rearrange("(p c) one -> p (c one)", p=128),
                                         in_=sent_t[:]), 16)
                idlh_done = sS2.n

                @G(STAGE >= 15, block.gpsimd)
                def _(g):
                    g.wait_ge(sS2.h, idlh_done)
                    sG.inc(g.indirect_dma_start(
                        out=wint[:], out_offset=None,
                        in_=idl_hb.rearrange("(a b) one -> a (b one)", b=128),
                        in_offset=bass.IndirectOffsetOnAxis(ap=widx[:], axis=0)), 16)
                win_g = sG.n


                @G(STAGE >= 15, block.sync)
                def _(sy):
                    sy.wait_ge(sG.h, win_g)
                    sS2.inc(sy.dma_start(out=win_hb.ap().rearrange("(j p) one -> j (p one)", p=128),
                                         in_=wint[:]), 16)
                    sy.wait_ge(sS2.h, sS2.n)
                    sS2.inc(sy.dma_start(out=winr_f[:],
                                         in_=win_hb.rearrange("(p c) one -> p (c one)", p=128)), 16)
                winr_ld = sS2.n

                @G(STAGE >= 15, block.vector)
                def _(v):
                    v.wait_ge(sS2.h, winr_ld)
                    v.tensor_scalar(winr_f[:], winr_f[:], float(T - 1), 0.0, ALU.min, ALU.max)
                    v.tensor_scalar(jkv[:, 1:2], jkv[:, 1:2], 1.0, None, ALU.mult)
                    v.tensor_copy(winr_i[:], winr_f[:])
                    vfin(v, sV)
                win_done = sV.n

                if DEBUG:
                    @block.sync
                    def _(sy):
                        sy.wait_ge(sS2.h, winr_ld)
                        sOUT.inc(sy.dma_start(out=dbg_win, in_=winr_f[:]), 16)
                        sy.wait_ge(sV.h, p5idx_done)
                        sOUT.inc(sy.dma_start(out=dbg_gate, in_=gate_cols[:]), 16)
                        sOUT.inc(sy.dma_start(out=dbg_gm, in_=gmask[:]), 16)


                    @block.gpsimd
                    def _(g):
                        g.wait_ge(sG.h, t1all_done)
                        sOUT.inc(g.dma_start(out=dbg_t1, in_=t1_all[:]), 16)
                        g.wait_ge(sS2.h, sm_ld)
                        sOUT.inc(g.dma_start(out=dbg_sm, in_=smine_h.ap()), 16)
                        g.wait_ge(sS2.h, idlh_done)
                        sOUT.inc(g.dma_start(out=dbg_idl, in_=idl_hb.ap()), 16)


            g3_done, st3_done = [], []
            ln3_v, ln3_s = [], []

            @block.gpsimd
            def _(g):
                g.wait_ge(sV.h, win_done)
                for c in (range(40) if STAGE >= 20 else []):
                    if c >= 2:
                        g.wait_ge(sST3.h, 16 * (c - 1))
                    sG3.inc(g.indirect_dma_start(
                        out=xgb[:, (c % 2) * D:(c % 2 + 1) * D], out_offset=None,
                        in_=x_all,
                        in_offset=bass.IndirectOffsetOnAxis(ap=winr_i[:, c:c + 1], axis=0)), 16)
                    g3_done.append(sG3.n)

            ln3_sm = {}

            def emit_ln3A(c):
                o = 8 * (c % 2)

                @block.scalar
                def _(sc, c=c, o=o):
                    sc.wait_ge(sG3.h, 16 * (c + 1))
                    if c >= 2:
                        sc.wait_ge(sST3.h, 16 * (c - 1))
                    sc.activation(xnc_b[:, (c % 2) * D:(c % 2 + 1) * D],
                                  xgb[:, (c % 2) * D:(c % 2 + 1) * D],
                                  AF.Square, accum_out=st3[:, o + 0:o + 1])
                    sfin(sc, sSc, op="square")
                ln3_s.append(sSc.n)

                @block.vector
                def _(v, c=c, o=o):
                    v.wait_ge(sSc.h, ln3_s[-1])
                    v.reduce_sum(st3[:, o + 1:o + 2], xgb[:, (c % 2) * D:(c % 2 + 1) * D], axis=AX.X)
                    v.tensor_scalar(st3[:, o + 0:o + 1], st3[:, o + 0:o + 1], 1.0 / D, None, ALU.mult)
                    v.tensor_scalar(st3[:, o + 1:o + 2], st3[:, o + 1:o + 2], 1.0 / D, None, ALU.mult)
                    v.tensor_scalar(jkv[:, 1:2], jkv[:, 1:2], 1.0, None, ALU.mult)
                    v.tensor_scalar(st3[:, o + 2:o + 3], st3[:, o + 1:o + 2], st3[:, o + 1:o + 2], None, ALU.mult)
                    v.tensor_scalar(jkv[:, 1:2], jkv[:, 1:2], 1.0, None, ALU.mult)
                    v.tensor_scalar(st3[:, o + 0:o + 1], st3[:, o + 0:o + 1], st3[:, o + 2:o + 3], None, ALU.subtract)
                    v.tensor_scalar(jkv[:, 1:2], jkv[:, 1:2], 1.0, None, ALU.mult)
                    v.tensor_scalar(st3[:, o + 0:o + 1], st3[:, o + 0:o + 1], EPS, None, ALU.add)
                    v.tensor_scalar(jkv[:, 1:2], jkv[:, 1:2], 1.0, None, ALU.mult)
                    v.reciprocal(st3[:, o + 0:o + 1], st3[:, o + 0:o + 1])
                    vfin(v, sV)
                ln3_sm[c] = sV.n

            def emit_ln3B(c):
                o = 8 * (c % 2)

                @block.scalar
                def _(sc, c=c, o=o):
                    sc.wait_ge(sV.h, ln3_sm[c])
                    sc.sqrt(st3[:, o + 3:o + 4], st3[:, o + 0:o + 1])
                    sc.mul(st3[:, o + 4:o + 5], st3[:, o + 1:o + 2], -1.0)
                    sc.mul(jks[:, 1:2], jks[:, 1:2], 1.0)
                    sc.mul(st3[:, o + 4:o + 5], st3[:, o + 4:o + 5], st3[:, o + 3:o + 4])
                    sfin(sc, sSc)
                ln3_s.append(sSc.n)

                @block.vector
                def _(v, c=c, o=o):
                    v.wait_ge(sSc.h, ln3_s[-1])
                    if c >= 2:
                        v.wait_ge(sST3.h, 16 * (c - 1))
                    v.tensor_scalar(xnc_b[:, (c % 2) * D:(c % 2 + 1) * D],
                                    xgb[:, (c % 2) * D:(c % 2 + 1) * D],
                                    st3[:, o + 3:o + 4], st3[:, o + 4:o + 5], ALU.mult, ALU.add)
                    vfin(v, sV)
                ln3_v.append(sV.n)

                @block.sync
                def _(sy, c=c):
                    sy.wait_ge(sV.h, ln3_v[-1])
                    sST3.inc(sy.dma_start(
                        out=xnc_hb.ap().rearrange("(p cc) d -> p cc d", cc=40)[:, c],
                        in_=xnc_b[:, (c % 2) * D:(c % 2 + 1) * D]), 16)
                st3_done.append(sST3.n)

            def emit_ln3(c):
                emit_ln3A(c)
                if c >= 1:
                    emit_ln3B(c - 1)

            with ExitStack() as ctx4:
                ent4 = ctx4.enter_context
                q_f = ent4(nc.sbuf_tensor("q_f", [128, NCH * P4T], F32))
                v_f = ent4(nc.sbuf_tensor("v_f", [128, NCH * P4T], F32))
                st_f = ent4(nc.sbuf_tensor("st_f", [128, NCH * P4T], F32))
                ybf = ent4(nc.sbuf_tensor("ybf", [128, NCH * P4T], BF16))
                ysq = ent4(nc.sbuf_tensor("ysq", [128, NCH * P4T], BF16))
                gs_b = ent4(nc.sbuf_tensor("gs_b", [128, 2 * NCH * P4T], BF16))
                mrow = ent4(nc.sbuf_tensor("mrow", [1, P4T], F32))
                isrow = ent4(nc.sbuf_tensor("isrow", [1, P4T], F32))
                s_b = ent4(nc.sbuf_tensor("s_b", [128, 2 * D], BF16))
                ps_p0 = ent4(nc.psum_tensor("ps_p0", [128, P4T], F32))
                ps_p1 = ent4(nc.psum_tensor("ps_p1", [128, P4T], F32))
                ps_s1 = ent4(nc.psum_tensor("ps_s1", [1, P4T], F32))
                ps_s2 = ent4(nc.psum_tensor("ps_s2", [1, P4T], F32))
                ps_mu = ent4(nc.psum_tensor("ps_mu", [128, P4T], F32))
                ps_is = ent4(nc.psum_tensor("ps_is", [128, P4T], F32))
                ps_wa = ent4(nc.psum_tensor("ps_wa", [128, 512], F32))
                ps_wb = ent4(nc.psum_tensor("ps_wb", [128, 512], F32))

                s4_marks = []
                sev_marks = []
                nln = 40 if STAGE >= 20 else 0
                ln3_ptr = 0
                GHALF = NCH * P4T
                pe_chain = []
                y_l, ysq_l, smm_l, mv_l, is_l, bc_l, yg_l, g_l = {}, {}, {}, {}, {}, {}, {}, {}

                def emit_epi1(ip):
                    @block.tensor
                    def _(t, ip=ip):
                        t.wait_ge(sV.h, y_l[ip])
                        t.wait_ge(sSc.h, ysq_l[ip])
                        for d in range(NCH):
                            t.matmul(ps_s1[:], onc[:], ybf[:, d * P4T:(d + 1) * P4T],
                                     start=(d == 0), stop=(d == NCH - 1))
                        for d in range(NCH):
                            ins = t.matmul(ps_s2[:], onc[:], ysq[:, d * P4T:(d + 1) * P4T],
                                           start=(d == 0), stop=(d == NCH - 1))
                        sT_.inc(ins)
                    smm_l[ip] = sT_.n

                    @block.vector
                    def _(v, ip=ip):
                        v.wait_ge(sT_.h, smm_l[ip])
                        v.tensor_scalar(mrow[:], ps_s1[:], 1.0 / D, None, ALU.mult)
                        v.tensor_scalar(isrow[:], ps_s2[:], 1.0 / D, None, ALU.mult)
                        v.tensor_tensor(ps_s1[:], mrow[:], mrow[:], ALU.mult)
                        v.tensor_scalar(jkv[:, 1:2], jkv[:, 1:2], 1.0, None, ALU.mult)
                        v.tensor_tensor(isrow[:], isrow[:], ps_s1[:], ALU.subtract)
                        v.tensor_scalar(isrow[:], isrow[:], EPS, None, ALU.add)
                        v.reciprocal(isrow[:], isrow[:])
                        vfin(v, sV)
                    mv_l[ip] = sV.n

                    @block.scalar
                    def _(sc, ip=ip):
                        sc.wait_ge(sV.h, mv_l[ip])
                        sc.sqrt(isrow[:], isrow[:])
                        sfin(sc, sSc)
                    is_l[ip] = sSc.n

                def emit_epi2(ip):
                    @block.tensor
                    def _(t, ip=ip):
                        t.wait_ge(sSc.h, is_l[ip])
                        t.matmul(ps_mu[:], onr[:], mrow[:], start=True, stop=True)
                        sT_.inc(t.matmul(ps_is[:], onr[:], isrow[:], start=True, stop=True))
                    bc_l[ip] = sT_.n

                    @block.vector
                    def _(v, ip=ip):
                        v.wait_ge(sT_.h, bc_l[ip])
                        v.wait_ge(sSc.h, g_l[ip])
                        for d in range(NCH):
                            v.tensor_copy(st_f[:, d * P4T:(d + 1) * P4T],
                                          ybf[:, d * P4T:(d + 1) * P4T])
                        for d in range(NCH):
                            v.tensor_tensor(st_f[:, d * P4T:(d + 1) * P4T],
                                            st_f[:, d * P4T:(d + 1) * P4T], ps_mu[:], ALU.subtract)
                        for d in range(NCH):
                            v.tensor_tensor(st_f[:, d * P4T:(d + 1) * P4T],
                                            st_f[:, d * P4T:(d + 1) * P4T], ps_is[:], ALU.mult)
                        for d in range(NCH):
                            v.tensor_scalar(ybf[:, d * P4T:(d + 1) * P4T],
                                            st_f[:, d * P4T:(d + 1) * P4T],
                                            gam_c[:, d:d + 1], bet_c[:, d:d + 1],
                                            ALU.mult, ALU.add)
                        v.tensor_tensor(ysq[:], ybf[:],
                                        gs_b[:, (ip % 2) * GHALF:(ip % 2 + 1) * GHALF], ALU.mult)
                        vfin(v, sV)
                    yg_l[ip] = sV.n

                def emit_wo(ip):
                    for sub in range(2):
                        @block.tensor
                        def _(t, ip=ip, sub=sub):
                            t.wait_ge(sV.h, yg_l[ip])
                            if sev_marks:
                                t.wait_ge(sSc.h, sev_marks[-1])
                            for d in range(NCH):
                                t.matmul(ps_wa[:], ysq[:, d * P4T + sub * 128:d * P4T + (sub + 1) * 128],
                                         wch(wo_sb, d)[:, 0:512], start=(d == 0), stop=(d == NCH - 1))
                            for d in range(NCH):
                                ins = t.matmul(ps_wb[:], ysq[:, d * P4T + sub * 128:d * P4T + (sub + 1) * 128],
                                               wch(wo_sb, d)[:, 512:1024], start=(d == 0), stop=(d == NCH - 1))
                            sT_.inc(ins)
                        wo_mm = sT_.n

                        @block.scalar
                        def _(sc, ip=ip, sub=sub, wo_mm=wo_mm):
                            sc.wait_ge(sT_.h, wo_mm)
                            idx = 2 * (ip - 1) + sub
                            if idx >= 0:
                                sc.wait_ge(sS4.h, s4_marks[idx])
                            sc.copy(s_b[:, sub * D:sub * D + 512], ps_wa[:])
                            sc.copy(s_b[:, sub * D + 512:(sub + 1) * D], ps_wb[:])
                            sfin(sc, sSc)
                        sev_marks.append(sSc.n)

                        @block.sync
                        def _(sy, ip=ip, sub=sub):
                            sy.wait_ge(sSc.h, sev_marks[-1])
                            sS4.inc(sy.dma_start(
                                out=s_sc.ap()[ip * P4T + sub * 128:ip * P4T + (sub + 1) * 128, :],
                                in_=s_b[:, sub * D:(sub + 1) * D]), 16)
                        s4_marks.append(sS4.n)

                NTOT = min(NP4, NP4_RUN) if STAGE >= 30 else 0

                def emit_proj(i, pj):
                    lo, hi = i * P4T, (i + 1) * P4T
                    wsb, dst, eng, act = ((wq_sb, q_f, "v", None),
                                          (wv_sb, v_f, "v", None),
                                          (wgt_sb, None, "s", AF.Silu))[pj]
                    if True:
                        for e in range(NCH):
                            psq = ps_p0 if e % 2 == 0 else ps_p1
                            gidx = len(pe_chain)

                            @block.tensor
                            def _(t, i=i, e=e, psq=psq, wsb=wsb, gidx=gidx):
                                if gidx == 0:
                                    t.wait_ge(sW.h, w_all)
                                if gidx >= 2:
                                    kind, mk = pe_chain[gidx - 2]
                                    t.wait_ge(sV.h if kind == "v" else sSc.h, mk)
                                for d in range(NCH):
                                    ins = t.matmul(psq[:], wch(wsb, d)[:, e * 128:(e + 1) * 128],
                                                   xch(d, lo, hi), start=(d == 0), stop=(d == NCH - 1))
                                sT_.inc(ins)
                            mmk = sT_.n

                            if eng == "v":
                                @block.vector
                                def _(v, i=i, e=e, pj=pj, psq=psq, dst=dst, mmk=mmk):
                                    v.wait_ge(sT_.h, mmk)
                                    if pj == 1 and e == 0 and i >= 1:
                                        v.wait_ge(sSc.h, ysq_l[i - 1])
                                    v.tensor_copy(dst[:, e * P4T:(e + 1) * P4T], psq[:])
                                    vfin(v, sV)
                                pe_chain.append(("v", sV.n))
                            else:
                                @block.scalar
                                def _(sc, i=i, e=e, psq=psq, mmk=mmk, act=act):
                                    sc.wait_ge(sT_.h, mmk)
                                    if e == 0 and i >= 2:
                                        sc.wait_ge(sV.h, yg_l[i - 2])
                                    sc.activation(gs_b[:, (i % 2) * GHALF + e * P4T:
                                                       (i % 2) * GHALF + (e + 1) * P4T], psq[:], act)
                                    sfin(sc, sSc, op="silu")
                                pe_chain.append(("s", sSc.n))
                        if pj == 2:
                            g_l[i] = sSc.n

                for i in range(NTOT):
                    emit_proj(i, 0)
                    if i >= 1:
                        emit_epi1(i - 1)
                    emit_proj(i, 1)
                    if i >= 1:
                        emit_epi2(i - 1)
                    emit_proj(i, 2)

                    @block.vector
                    def _(v, i=i):
                        for d in range(NCH):
                            init = 0.0 if i % (NP4 // 2) == 0 else carry[:, d:d + 1]
                            v.tensor_tensor_scan(st_f[:, d * P4T:(d + 1) * P4T],
                                                 lam_bt[:, d * P4T:(d + 1) * P4T],
                                                 v_f[:, d * P4T:(d + 1) * P4T],
                                                 init, ALU.mult, ALU.add)
                        for d in range(NCH):
                            v.tensor_copy(carry[:, d:d + 1], st_f[:, d * P4T + P4T - 1:d * P4T + P4T])
                        v.tensor_tensor(v_f[:], q_f[:], st_f[:], ALU.mult)
                        v.tensor_copy(ybf[:], v_f[:])
                        vfin(v, sV)
                    y_l[i] = sV.n

                    if i >= 1:
                        emit_wo(i - 1)

                    @block.scalar
                    def _(sc, i=i):
                        sc.wait_ge(sV.h, y_l[i])
                        sc.activation(ysq[:], v_f[:], AF.Square)
                        sfin(sc, sSc, op="square")
                    ysq_l[i] = sSc.n

                    for _k in range(3):
                        if ln3_ptr < nln:
                            emit_ln3(ln3_ptr)
                            ln3_ptr += 1
                if NTOT >= 1:
                    emit_epi1(NTOT - 1)
                    emit_epi2(NTOT - 1)
                    emit_wo(NTOT - 1)
                while ln3_ptr < nln:
                    emit_ln3(ln3_ptr)
                    ln3_ptr += 1
                if nln >= 1:
                    emit_ln3B(nln - 1)

                if DEBUG:
                    @block.gpsimd
                    def _(g):
                        if s4_marks:
                            g.wait_ge(sS4.h, s4_marks[-1])
                            sOUT.inc(g.dma_start(out=dbg_s, in_=s_sc.ap()), 16)
                            if NP4_RUN == 1:
                                sOUT.inc(g.dma_start(out=dbg_q, in_=q_f[:]), 16)
                                sOUT.inc(g.dma_start(out=dbg_v, in_=v_f[:]), 16)
                                sOUT.inc(g.dma_start(out=dbg_st, in_=st_f[:]), 16)
                                sOUT.inc(g.dma_start(out=dbg_yg, in_=ysq[:]), 16)
                        g.wait_ge(sXNF.h, sXNF.n)
                        sOUT.inc(g.dma_start(out=dbg_xnfm, in_=xn_fm[:, 3 * NT:4 * NT]), 16)
                        sOUT.inc(g.dma_start(out=dbg_xnsc, in_=xn_sc.ap()), 16)
                        if st3_done:
                            g.wait_ge(sST3.h, st3_done[-1])
                            sOUT.inc(g.dma_start(out=dbg_xnc, in_=xnc_hb.ap()), 16)


        with ExitStack() as ctx3:
            ent3 = ctx3.enter_context
            bufT = ent3(nc.sbuf_tensor("bufT", [128, 2 * NCH * CAP], BF16))
            wg_e = ent3(nc.sbuf_tensor("wg_e", [128, 2 * NCH * D], BF16))
            wu_e = ent3(nc.sbuf_tensor("wu_e", [128, 2 * NCH * D], BF16))
            wd_e = ent3(nc.sbuf_tensor("wd_e", [128, NCH * D], BF16))
            mid = ent3(nc.sbuf_tensor("mid", [128, NCH * CAP], BF16))
            sg_t = ent3(nc.sbuf_tensor("sg_t", [128, 512], BF16))
            ub_t = ent3(nc.sbuf_tensor("ub_t", [128, 512], BF16))
            yb3 = ent3(nc.sbuf_tensor("yb3", [128, D], BF16))
            ps_g = ent3(nc.psum_tensor("ps_g", [128, 512], F32))
            ps_u = ent3(nc.psum_tensor("ps_u", [128, 512], F32))
            ps_y0 = ent3(nc.psum_tensor("ps_y0", [128, 512], F32))
            ps_y1 = ent3(nc.psum_tensor("ps_y1", [128, 512], F32))

            WIN = ((0, 512), (512, 1024), (1024, 1280))
            gu_done, y3_done, td_marks = [], [], []
            we_marks = []
            for j in (range(NE) if STAGE >= 40 else []):
                @block.sync
                def _(sy, j=j):
                    sy.wait_ge(sST3.h, 16 * 40)
                    if j == 0 and s4_marks:
                        sy.wait_ge(sS4.h, s4_marks[-1])
                    if j >= 2:
                        sy.wait_ge(sT_.h, gu_done[j - 2])
                    for h in range(NCH):
                        sTD.inc(sy.dma_start(
                            out=bufT[:, (j % 2) * NCH * CAP + h * CAP:(j % 2) * NCH * CAP + (h + 1) * CAP],
                            in_=xnc_hb.ap()[j * CAP:(j + 1) * CAP, h * 128:(h + 1) * 128],
                            transpose=True), 16)
                td_marks.append(sTD.n)

                @block.gpsimd
                def _(g, j=j):
                    if j == 0 and s4_marks:
                        g.wait_ge(sS4.h, s4_marks[-1])
                    if j >= 2:
                        g.wait_ge(sT_.h, gu_done[j - 2])
                    sWE.inc(g.dma_start(out=wg_e[:].rearrange("p (b c d) -> p b c d", b=2, c=NCH)[:, j % 2],
                                        in_=weg.rearrange("j (c p) d -> j p c d", p=128)[j]), 16)
                    sWE.inc(g.dma_start(out=wu_e[:].rearrange("p (b c d) -> p b c d", b=2, c=NCH)[:, j % 2],
                                        in_=weu.rearrange("j (c p) d -> j p c d", p=128)[j]), 16)
                    if j >= 1:
                        g.wait_ge(sT_.h, y3_done[j - 1])
                    sWE.inc(g.dma_start(out=wd_e[:].rearrange("p (c d) -> p c d", c=NCH),
                                        in_=wed.rearrange("j (c p) d -> j p c d", p=128)[j]), 16)
                we_marks.append(sWE.n)

                bT = lambda h, lo, hi, j=j: bufT[:, (j % 2) * NCH * CAP + h * CAP + lo:
                                                 (j % 2) * NCH * CAP + h * CAP + hi]
                ev3_marks = []
                for (wlo, whi) in WIN:
                    for f in range(NCH):
                        @block.tensor
                        def _(t, j=j, f=f, wlo=wlo, whi=whi):
                            t.wait_ge(sTD.h, td_marks[j])
                            t.wait_ge(sWE.h, we_marks[j] - 16)
                            if ev3_marks:
                                t.wait_ge(sV.h, ev3_marks[-1])
                            for h in range(NCH):
                                t.matmul(ps_g[:, 0:whi - wlo],
                                         wg_e[:].rearrange("p (b c d) -> p b c d", b=2, c=NCH)[:, j % 2, h,
                                                                                               f * 128:(f + 1) * 128]
                                         if False else
                                         wg_e[:, (j % 2) * NCH * D + h * D + f * 128:
                                              (j % 2) * NCH * D + h * D + (f + 1) * 128],
                                         bT(h, wlo, whi), start=(h == 0), stop=(h == NCH - 1))
                            for h in range(NCH):
                                ins = t.matmul(ps_u[:, 0:whi - wlo],
                                               wu_e[:, (j % 2) * NCH * D + h * D + f * 128:
                                                    (j % 2) * NCH * D + h * D + (f + 1) * 128],
                                               bT(h, wlo, whi), start=(h == 0), stop=(h == NCH - 1))
                            sT_.inc(ins)
                        gu_mm = sT_.n

                        @block.scalar
                        def _(sc, gu_mm=gu_mm, wlo=wlo, whi=whi):
                            sc.wait_ge(sT_.h, gu_mm)
                            sc.activation(sg_t[:, 0:whi - wlo], ps_g[:, 0:whi - wlo], AF.Silu)
                            sfin(sc, sSc, op="silu")
                        sg_done = sSc.n

                        @block.vector
                        def _(v, f=f, wlo=wlo, whi=whi, gu_mm=gu_mm, sg_done=sg_done):
                            v.wait_ge(sT_.h, gu_mm)
                            v.wait_ge(sSc.h, sg_done)
                            v.tensor_copy(ub_t[:, 0:whi - wlo], ps_u[:, 0:whi - wlo])
                            v.tensor_scalar(jkv[:, 1:2], jkv[:, 1:2], 1.0, None, ALU.mult)
                            v.tensor_tensor(mid[:, f * CAP + wlo:f * CAP + whi],
                                            sg_t[:, 0:whi - wlo], ub_t[:, 0:whi - wlo], ALU.mult)
                            vfin(v, sV)
                        ev3_marks.append(sV.n)
                gu_done.append(sT_.n)

                yev_marks = []
                for sc_ in range(10):
                    @block.tensor
                    def _(t, j=j, sc_=sc_):
                        t.wait_ge(sV.h, ev3_marks[-1])
                        t.wait_ge(sWE.h, we_marks[j])
                        if yev_marks:
                            t.wait_ge(sV.h, yev_marks[-1])
                        for f in range(NCH):
                            t.matmul(ps_y0[:], mid[:, f * CAP + sc_ * 128:f * CAP + (sc_ + 1) * 128],
                                     wd_e[:, f * D:f * D + 512], start=(f == 0), stop=(f == NCH - 1))
                        for f in range(NCH):
                            ins = t.matmul(ps_y1[:], mid[:, f * CAP + sc_ * 128:f * CAP + (sc_ + 1) * 128],
                                           wd_e[:, f * D + 512:(f + 1) * D], start=(f == 0), stop=(f == NCH - 1))
                        sT_.inc(ins)
                    ymm = sT_.n

                    @block.vector
                    def _(v, ymm=ymm):
                        v.wait_ge(sT_.h, ymm)
                        v.tensor_copy(yb3[:, 0:512], ps_y0[:])
                        v.tensor_copy(yb3[:, 512:1024], ps_y1[:])
                        vfin(v, sV)
                    yev_marks.append(sV.n)

                    @block.sync
                    def _(sy, j=j, sc_=sc_):
                        sy.wait_ge(sV.h, yev_marks[-1])
                        sY3.inc(sy.dma_start(
                            out=ag3_in.ap()[j * CAP + sc_ * 128:j * CAP + (sc_ + 1) * 128, :],
                            in_=yb3[:]), 16)
                y3_done.append(sT_.n)

            if STAGE >= 40:
                @block.gpsimd
                def _(g):
                    g.wait_ge(sY3.h, sY3.n)
                    sCC.inc(g.collective_compute(
                        "AllGather", ALU.bypass, replica_groups=[list(range(NCORE))],
                        ins=[ag3_in.ap()], outs=[ag3_out.ap()]))
            ag3_done = sCC.n

            if DEBUG and STAGE >= 40:
                @block.gpsimd
                def _(g):
                    g.wait_ge(sY3.h, sY3.n)
                    sOUT.inc(g.dma_start(out=dbg_y3, in_=ag3_in.ap()), 16)

        with ExitStack() as ctx5:
            ent5 = ctx5.enter_context
            PD = 3
            mg = ent5(nc.sbuf_tensor("mg", [128, PD * D], BF16))
            xrow = ent5(nc.sbuf_tensor("xrow", [128, PD * D], F32))
            xnr = ent5(nc.sbuf_tensor("xnr", [128, PD * D], BF16))
            srow = ent5(nc.sbuf_tensor("srow", [128, PD * D], BF16))
            of32 = ent5(nc.sbuf_tensor("of32", [128, PD * D], F32))
            xf32 = ent5(nc.sbuf_tensor("xf32", [128, PD * D], F32))
            gs5 = ent5(nc.sbuf_tensor("gs5", [128, PD * D], BF16))

            p5v, p5o = [], []

            def emit_store(jj):
                b = jj % PD

                @block.sync
                def _(sy, jj=jj, b=b):
                    sy.wait_ge(sV.h, p5v[jj])
                    sOUT.inc(sy.dma_start(out=y_out[jj * 128:(jj + 1) * 128, :],
                                          in_=of32[:, b * D:(b + 1) * D]), 16)
                p5o.append(sOUT.n)

            for jj in (range(32) if STAGE >= 50 else []):
                b = jj % PD

                @block.gpsimd
                def _(g, jj=jj, b=b):
                    g.wait_ge(sCC.h, ag3_done)
                    g.wait_ge(sV.h, p5idx_done)
                    if jj >= PD:
                        g.wait_ge(sV.h, p5v[jj - PD])
                    sG5.inc(g.indirect_dma_start(
                        out=mg[:, b * D:(b + 1) * D], out_offset=None,
                        in_=ag3_out.ap(),
                        in_offset=bass.IndirectOffsetOnAxis(ap=slotT_i[:, jj:jj + 1], axis=0)), 16)

                @block.sync
                def _(sy, jj=jj, b=b):
                    if jj >= PD:
                        sy.wait_ge(sV.h, p5v[jj - PD])
                    sy.wait_ge(sS4.h, s4_marks[jj])
                    sP5.inc(sy.dma_start(out=xrow[:, b * D:(b + 1) * D],
                                         in_=x_own[jj * 128:(jj + 1) * 128, :]), 16)
                    sP5.inc(sy.dma_start(out=xnr[:, b * D:(b + 1) * D],
                                         in_=xn_sc.ap()[jj * 128:(jj + 1) * 128, :]), 16)
                    sP5.inc(sy.dma_start(out=srow[:, b * D:(b + 1) * D],
                                         in_=s_sc.ap()[jj * 128:(jj + 1) * 128, :]), 16)
                if jj >= PD:
                    emit_store(jj - PD)

                @block.vector
                def _(v, jj=jj, b=b):
                    v.wait_ge(sG5.h, 16 * (jj + 1))
                    v.wait_ge(sP5.h, 48 * (jj + 1))
                    if jj >= PD:
                        v.wait_ge(sOUT.h, p5o[jj - PD])
                    v.tensor_scalar(of32[:, b * D:(b + 1) * D], mg[:, b * D:(b + 1) * D],
                                    gmask[:, jj:jj + 1], None, ALU.mult)
                    v.tensor_copy(xf32[:, b * D:(b + 1) * D], xnr[:, b * D:(b + 1) * D])
                    v.tensor_tensor(of32[:, b * D:(b + 1) * D], of32[:, b * D:(b + 1) * D],
                                    xf32[:, b * D:(b + 1) * D], ALU.add)
                    vfin(v, sV)
                ffn_done = sV.n

                @block.scalar
                def _(sc, b=b, ffn_done=ffn_done):
                    sc.wait_ge(sV.h, ffn_done)
                    sc.activation(gs5[:, b * D:(b + 1) * D], of32[:, b * D:(b + 1) * D], AF.Silu)
                    sfin(sc, sSc, op="silu")
                g5_done = sSc.n

                @block.vector
                def _(v, b=b, g5_done=g5_done):
                    v.wait_ge(sSc.h, g5_done)
                    v.tensor_tensor(xf32[:, b * D:(b + 1) * D], gs5[:, b * D:(b + 1) * D],
                                    srow[:, b * D:(b + 1) * D], ALU.mult)
                    v.tensor_scalar(jkv[:, 1:2], jkv[:, 1:2], 1.0, None, ALU.mult)
                    v.tensor_tensor(of32[:, b * D:(b + 1) * D], xf32[:, b * D:(b + 1) * D],
                                    xrow[:, b * D:(b + 1) * D], ALU.add)
                    vfin(v, sV)
                p5v.append(sV.n)

            for jj in (range(32 - PD, 32) if STAGE >= 50 else []):
                emit_store(jj)

            allc = [sOUT, sST3, sV, sSc, sT_, sG, sG3, sCC, sS2, sXNF, sS4,
                    sY3, sTD, sWE, sG5, sP5, sX, sXN, sW, sL3]

            @block.sync
            def _(sy):
                for c in allc:
                    if c.n:
                        sy.wait_ge(c.h, c.n)

            @block.gpsimd
            def _(g):
                for c in allc:
                    if c.n:
                        g.wait_ge(c.h, c.n)

    return nc


_NC_CACHE = {}


def kernel(L0_attn_x, L0_attn_decay, L0_attn_Wq, L0_attn_Wv, L0_attn_Wo, L0_attn_Wg,
           L0_attn_ln_gamma, L0_attn_ln_beta, L0_ffn_w_router, L0_ffn_w_gate,
           L0_ffn_w_up, L0_ffn_w_down, _debug=False):
    x = np.ascontiguousarray(np.asarray(L0_attn_x, dtype=np.float32).reshape(T, D))
    wq_t = np.ascontiguousarray(np.asarray(L0_attn_Wq, np.float32).T)
    wv_t = np.ascontiguousarray(np.asarray(L0_attn_Wv, np.float32).T)
    wg_t = np.ascontiguousarray(np.asarray(L0_attn_Wg, np.float32).T)
    wo_t = np.ascontiguousarray(np.asarray(L0_attn_Wo, np.float32).T)
    wr = np.ascontiguousarray(np.asarray(L0_ffn_w_router, np.float32))
    weg_a = np.asarray(L0_ffn_w_gate, np.float32)
    weu_a = np.asarray(L0_ffn_w_up, np.float32)
    wed_a = np.asarray(L0_ffn_w_down, np.float32)
    gam = np.ascontiguousarray(np.asarray(L0_attn_ln_gamma, np.float32).reshape(NCH, 128).T)
    bet = np.ascontiguousarray(np.asarray(L0_attn_ln_beta, np.float32).reshape(NCH, 128).T)
    lam = np.ascontiguousarray(np.repeat(np.asarray(L0_attn_decay, np.float32), HD).reshape(NCH, 128).T)
    lamb_bt = np.ascontiguousarray(np.repeat(lam, P4T, axis=1))

    iotae = np.tile(np.arange(E, dtype=np.float32), (128, 1))
    c32a = np.tile(np.arange(E, dtype=np.float32) - E, (128, 1))
    identa = np.eye(128, dtype=np.float32)
    ltua = np.triu(np.ones((128, 128), np.float32), 1).astype(ml_dtypes.bfloat16)
    onca = np.ones((128, 1), ml_dtypes.bfloat16)
    onra = np.ones((1, 128), np.float32)

    global _NC_CACHE
    key = (bool(DEBUG), STAGE)
    if key not in _NC_CACHE:
        _NC_CACHE[key] = build_nc()
    nc = _NC_CACHE[key]

    in_maps = []
    for k in range(NCORE):
        p_cc, cc_i = np.meshgrid(np.arange(128), np.arange(32), indexing="ij")
        ids = (NT * k + p_cc * 32 + cc_i).astype(np.float32)
        widxa = (40 * k + np.arange(40, dtype=np.int32)).reshape(40, 1)
        perma = np.zeros((128, 128), np.float32)
        for m in range(16):
            perma[16 * k + m, m] = 1.0
        in_maps.append({
            "x_own": x[NT * k:NT * (k + 1)],
            "x_all": x,
            "wq_t": wq_t, "wv_t": wv_t, "wg_t": wg_t, "wo_t": wo_t,
            "wr_h": wr,
            "weg": np.ascontiguousarray(weg_a[NE * k:NE * (k + 1)]),
            "weu": np.ascontiguousarray(weu_a[NE * k:NE * (k + 1)]),
            "wed": np.ascontiguousarray(wed_a[NE * k:NE * (k + 1)]),
            "gam_h": gam, "bet_h": bet, "lam_h": lam, "lamb_h": lamb_bt,
            "ids_h": ids, "widx_h": widxa, "perm_h": perma,
            "iotae_h": iotae, "c32_h": c32a, "ident_h": identa,
            "ltu_h": ltua, "onc_h": onca, "onr_h": onra,
        })
    res = run_bass_kernel_spmd(nc, in_maps, core_ids=list(range(NCORE)))
    out = np.concatenate([res.results[k]["y_out"] for k in range(NCORE)], axis=0)
    out = out.reshape(B, S, D)
    if _debug:
        return out, res
    return out


# version 10
# speedup vs baseline: 1.0563x; 1.0563x over previous
import numpy as np
import ml_dtypes
from contextlib import ExitStack
import concourse.bass as bass
import concourse.mybir as mybir
from concourse.bass_utils import run_bass_kernel_spmd

F32 = mybir.dt.float32
BF16 = mybir.dt.bfloat16
I32 = mybir.dt.int32
AF = mybir.ActivationFunctionType
ALU = mybir.AluOpType
AX = mybir.AxisListType

B, S, D, H, HD = 16, 2048, 1024, 16, 64
E, CAP = 32, 1280
T = B * S
NCORE = 8
NT = T // NCORE
NE = E // NCORE
NSLOT = NE * CAP
TBL = 41088
DUMP = 40960.0
SENT = 1.0e9
NCH = D // 128
P1T = 512
NP1 = NT // P1T
P4T = 256
NP4 = NT // P4T
EPS = 1e-5
DEBUG = False
STAGE = 99
NP1_RUN = 8
NP4_RUN = 16


def G(cond, reg):
    def deco(fn):
        if cond:
            reg(fn)
    return deco


class Ctr:
    def __init__(self, h):
        self.h = h
        self.n = 0

    def inc(self, instr, k=1):
        instr.then_inc(self.h, k)
        self.n += k
        return self.n


def build_nc():
    nc = bass.Bass("TRN2", target_bir_lowering=False, debug=False)

    x_own = nc.dram_tensor("x_own", [NT, D], F32, kind="ExternalInput").ap()
    x_all = nc.dram_tensor("x_all", [T, D], F32, kind="ExternalInput").ap()
    wq_t = nc.dram_tensor("wq_t", [D, D], F32, kind="ExternalInput").ap()
    wv_t = nc.dram_tensor("wv_t", [D, D], F32, kind="ExternalInput").ap()
    wg_t = nc.dram_tensor("wg_t", [D, D], F32, kind="ExternalInput").ap()
    wo_t = nc.dram_tensor("wo_t", [D, D], F32, kind="ExternalInput").ap()
    wr_h = nc.dram_tensor("wr_h", [D, E], F32, kind="ExternalInput").ap()
    weg = nc.dram_tensor("weg", [NE, D, D], F32, kind="ExternalInput").ap()
    weu = nc.dram_tensor("weu", [NE, D, D], F32, kind="ExternalInput").ap()
    wed = nc.dram_tensor("wed", [NE, D, D], F32, kind="ExternalInput").ap()
    gam_h = nc.dram_tensor("gam_h", [128, NCH], F32, kind="ExternalInput").ap()
    bet_h = nc.dram_tensor("bet_h", [128, NCH], F32, kind="ExternalInput").ap()
    lam_h = nc.dram_tensor("lam_h", [128, NCH], F32, kind="ExternalInput").ap()
    lamb_h = nc.dram_tensor("lamb_h", [128, NCH * P4T], F32, kind="ExternalInput").ap()
    ids_h = nc.dram_tensor("ids_h", [128, 32], F32, kind="ExternalInput").ap()
    widx_h = nc.dram_tensor("widx_h", [40, 1], I32, kind="ExternalInput").ap()
    perm_h = nc.dram_tensor("perm_h", [128, 128], F32, kind="ExternalInput").ap()
    iotae_h = nc.dram_tensor("iotae_h", [128, E], F32, kind="ExternalInput").ap()
    c32_h = nc.dram_tensor("c32_h", [128, E], F32, kind="ExternalInput").ap()
    ident_h = nc.dram_tensor("ident_h", [128, 128], F32, kind="ExternalInput").ap()
    ltu_h = nc.dram_tensor("ltu_h", [128, 128], BF16, kind="ExternalInput").ap()
    onc_h = nc.dram_tensor("onc_h", [128, 1], BF16, kind="ExternalInput").ap()
    onr_h = nc.dram_tensor("onr_h", [1, 128], F32, kind="ExternalInput").ap()

    y_out = nc.dram_tensor("y_out", [NT, D], F32, kind="ExternalOutput").ap()
    if DEBUG:
        dbg_t1 = nc.dram_tensor("dbg_t1", [128, 256], F32, kind="ExternalOutput").ap()
        dbg_slot = nc.dram_tensor("dbg_slot", [128, 256], F32, kind="ExternalOutput").ap()
        dbg_pos = nc.dram_tensor("dbg_pos", [128, 256], F32, kind="ExternalOutput").ap()
        dbg_idl = nc.dram_tensor("dbg_idl", [TBL, 1], F32, kind="ExternalOutput").ap()
        dbg_win = nc.dram_tensor("dbg_win", [128, 40], F32, kind="ExternalOutput").ap()
        dbg_xnc = nc.dram_tensor("dbg_xnc", [NSLOT, D], F32, kind="ExternalOutput").ap()
        dbg_gate = nc.dram_tensor("dbg_gate", [128, 32], F32, kind="ExternalOutput").ap()
        dbg_gm = nc.dram_tensor("dbg_gm", [128, 32], F32, kind="ExternalOutput").ap()
        dbg_xnfm = nc.dram_tensor("dbg_xnfm", [128, NT], F32, kind="ExternalOutput").ap()
        dbg_xnsc = nc.dram_tensor("dbg_xnsc", [NT, D], F32, kind="ExternalOutput").ap()
        dbg_sm = nc.dram_tensor("dbg_sm2", [NT, 1], F32, kind="ExternalOutput").ap()

        dbg_ps0 = nc.dram_tensor("dbg_ps0", [32, P1T], F32, kind="ExternalOutput").ap()
        dbg_q = nc.dram_tensor("dbg_q", [128, NCH * P4T], F32, kind="ExternalOutput").ap()
        dbg_v = nc.dram_tensor("dbg_v", [128, NCH * P4T], F32, kind="ExternalOutput").ap()
        dbg_st = nc.dram_tensor("dbg_st", [128, NCH * P4T], F32, kind="ExternalOutput").ap()
        dbg_yg = nc.dram_tensor("dbg_yg", [128, NCH * P4T], F32, kind="ExternalOutput").ap()
        dbg_s = nc.dram_tensor("dbg_s", [NT, D], F32, kind="ExternalOutput").ap()
        dbg_y3 = nc.dram_tensor("dbg_y3", [NSLOT, D], F32, kind="ExternalOutput").ap()

    xn_sc = nc.dram_tensor("xn_sc", [NT, D], BF16)
    s_sc = nc.dram_tensor("s_sc", [NT, D], BF16)
    ag1_in = nc.dram_tensor("ag1_in", [NT, 1], BF16)
    ag1_out = nc.dram_tensor("ag1_out", [T, 1], BF16, addr_space="Shared")
    ptab = nc.dram_tensor("ptab", [TBL, 1], F32)
    ag2_out = nc.dram_tensor("ag2_out", [NCORE * TBL, 1], F32, addr_space="Shared")
    idl_hb = nc.dram_tensor("idl_hb", [TBL, 1], F32)
    smine_h = nc.dram_tensor("smine_h", [NT, 1], F32)
    win_hb = nc.dram_tensor("win_hb", [NSLOT, 1], F32)
    xnc_hb = nc.dram_tensor("xnc_hb", [NSLOT, D], BF16)
    ag3_in = nc.dram_tensor("ag3_in", [NSLOT, D], BF16)
    ag3_out = nc.dram_tensor("ag3_out", [E * CAP, D], BF16, addr_space="Shared")

    with ExitStack() as ctx:
        ent = ctx.enter_context
        wr_hi = ent(nc.sbuf_tensor("wr_hi", [128, NCH * E], BF16))
        gam_c = ent(nc.sbuf_tensor("gam_c", [128, NCH], F32))
        bet_c = ent(nc.sbuf_tensor("bet_c", [128, NCH], F32))
        lam_c = ent(nc.sbuf_tensor("lam_c", [128, NCH], F32))
        lam_bt = ent(nc.sbuf_tensor("lam_bt", [128, NCH * P4T], F32))
        ids_f = ent(nc.sbuf_tensor("ids_f", [128, 32], F32))
        widx = ent(nc.sbuf_tensor("widx", [40, 1], I32))
        permk = ent(nc.sbuf_tensor("permk", [128, 128], F32))
        iotae = ent(nc.sbuf_tensor("iotae", [128, E], F32))
        c32 = ent(nc.sbuf_tensor("c32", [128, E], F32))
        ident = ent(nc.sbuf_tensor("ident", [128, 128], F32))
        ltu = ent(nc.sbuf_tensor("ltu", [128, 128], BF16))
        onc = ent(nc.sbuf_tensor("onc", [128, 1], BF16))
        onr = ent(nc.sbuf_tensor("onr", [1, 128], F32))
        gate_cols = ent(nc.sbuf_tensor("gate_cols", [128, 32], F32))
        top_cols = ent(nc.sbuf_tensor("top_cols", [128, 32], F32))
        sent_t = ent(nc.sbuf_tensor("sent_t", [128, 321], F32))
        t1_all = ent(nc.sbuf_tensor("t1_all", [128, 256], BF16))
        wint = ent(nc.sbuf_tensor("wint", [40, 128], F32))
        winr_i = ent(nc.sbuf_tensor("winr_i", [128, 40], I32))
        slotT_i = ent(nc.sbuf_tensor("slotT_i", [128, 32], I32))
        gmask = ent(nc.sbuf_tensor("gmask", [128, 32], F32))
        carry = ent(nc.sbuf_tensor("carry", [128, NCH], F32))
        xgb = ent(nc.sbuf_tensor("xgb", [128, 2 * D], F32))
        xnc_b = ent(nc.sbuf_tensor("xnc_b", [128, 2 * D], BF16))
        st3 = ent(nc.sbuf_tensor("st3", [128, 8], F32))
        jkv = ent(nc.sbuf_tensor("jkv", [128, 8], F32))
        jks = ent(nc.sbuf_tensor("jks", [128, 8], F32))
        jkb = ent(nc.sbuf_tensor("jkb", [128, 1], BF16))
        slm = ent(nc.sbuf_tensor("slm", [128, 32], F32))
        slm_i = ent(nc.sbuf_tensor("slm_i", [128, 32], I32))
        smT = ent(nc.sbuf_tensor("smT", [32, 128], F32))
        t1t = ent(nc.sbuf_tensor("t1t", [32, 128], BF16))
        winr_f = ent(nc.sbuf_tensor("winr_f", [128, 40], F32))
        pblk = ent(nc.sbuf_tensor("pblk", [128, 2 * 321], F32))

        def vfin(v, ctr, k=1):
            v.tensor_scalar(jkv[:, 0:1], jkv[:, 0:1], 1.0, None, ALU.mult)
            ctr.inc(v.drain(), k)

        def sfin(sc, ctr, k=1, op=None):
            if op == "silu":
                sc.activation(jkb[:], jks[:, 0:1], AF.Silu)
            elif op == "square":
                sc.activation(jkb[:], jks[:, 0:1], AF.Square)
            else:
                sc.mul(jks[:, 0:1], jks[:, 0:1], 1.0)
            ctr.inc(sc.drain(), k)

        names = ("sW sX sXN sXNF sT sV sSc sG sCC sS2 sG3 sL3 sST3 sWE sTD "
                 "sY3 sS4 sP5 sG5 sOUT").split()
        cs = {k: Ctr(ent(nc.semaphore(k))) for k in names}
        sW, sX, sXN, sXNF, sT_, sV, sSc, sG, sCC, sS2, sG3, sL3, sST3, sWE, sTD, \
            sY3, sS4, sP5, sG5, sOUT = (cs[k] for k in names)

        block = ent(nc.Block())

        @block.sync
        def _(sy):
            for dst, src in ((gam_c, gam_h), (bet_c, bet_h), (lam_c, lam_h)):
                sW.inc(sy.dma_start(out=dst[:], in_=src), 16)
            sW.inc(sy.dma_start(out=lam_bt[:], in_=lamb_h), 16)
            for dst, src in ((ids_f, ids_h), (widx, widx_h), (permk, perm_h),
                             (iotae, iotae_h), (c32, c32_h), (ident, ident_h),
                             (ltu, ltu_h), (onc, onc_h), (onr, onr_h)):
                sW.inc(sy.dma_start(out=dst[:], in_=src), 16)
        w_consts = sW.n

        @block.scalar
        def _(sc):
            sc.memzero(jks[:])

        @block.vector
        def _(v):
            v.memset(jkv[:], 0.0)
            v.memset(sent_t[:], SENT)
            vfin(v, sV)
        v_p0 = sV.n

        @block.sync
        def _(sy):
            sy.wait_ge(sV.h, v_p0)
            sS2.inc(sy.dma_start(out=ptab.ap().rearrange("(p c) one -> p (c one)", p=128),
                                 in_=sent_t[:]), 16)

        s2_sent = sS2.n

        with ExitStack() as ctxA:
            entA = ctxA.enter_context
            wq_sb = entA(nc.sbuf_tensor("wq_sb", [128, NCH * D], BF16))
            wv_sb = entA(nc.sbuf_tensor("wv_sb", [128, NCH * D], BF16))
            wgt_sb = entA(nc.sbuf_tensor("wgt_sb", [128, NCH * D], BF16))
            wo_sb = entA(nc.sbuf_tensor("wo_sb", [128, NCH * D], BF16))
            xn_fm = entA(nc.sbuf_tensor("xn_fm", [128, NCH * NT], BF16))

            def wch(wsb, d):
                return wsb[:].rearrange("p (c d) -> p c d", c=NCH)[:, d]

            def xch(d, lo, hi):
                return xn_fm[:, d * NT + lo: d * NT + hi]

            @block.gpsimd
            def _(g):
                sW.inc(g.dma_start(out=wr_hi[:].rearrange("p (c e) -> p c e", c=NCH),
                                   in_=wr_h.rearrange("(c p) e -> p c e", p=128)), 16)
                for w_h, w_sb in ((wq_t, wq_sb), (wv_t, wv_sb), (wg_t, wgt_sb), (wo_t, wo_sb)):
                    sW.inc(g.dma_start(out=w_sb[:].rearrange("p (c d) -> p c d", c=NCH),
                                       in_=w_h.rearrange("(c p) d -> p c d", p=128)), 16)
            w_all = sW.n

            with ExitStack() as ctx1:
                ent1 = ctx1.enter_context
                xr = ent1(nc.sbuf_tensor("xr", [128, 4 * D], F32))
                xn_tok = ent1(nc.sbuf_tensor("xn_tok", [128, 4 * D], BF16))
                lsb = ent1(nc.sbuf_tensor("lsb", [32, P1T], F32))
                lgt = ent1(nc.sbuf_tensor("lgt", [128, 4 * E], F32))
                lge = ent1(nc.sbuf_tensor("lge", [128, 4 * E], F32))
                eqb = ent1(nc.sbuf_tensor("eqb", [128, 4 * E], F32))
                rc = ent1(nc.sbuf_tensor("rc", [128, 24], F32))
                slotf = ent1(nc.sbuf_tensor("slotf", [128, 32], F32))
                tot = ent1(nc.sbuf_tensor("tot", [128, E], F32))
                tot_bf = ent1(nc.sbuf_tensor("tot_bf", [128, E], BF16))
                rp_sb = ent1(nc.sbuf_tensor("rp_sb", [128, E], F32))
                ones_t = ent1(nc.sbuf_tensor("ones_t", [128, 256], F32))
                rank = ent1(nc.sbuf_tensor("rank", [128, 256], F32))
                ohb = ent1(nc.sbuf_tensor("ohb", [128, 256], F32))
                scn = ent1(nc.sbuf_tensor("scn", [128, 256], F32))
                pos_t = ent1(nc.sbuf_tensor("pos_t", [128, 256], F32))
                slot_t = ent1(nc.sbuf_tensor("slot_t", [128, 256], F32))
                ps_r = ent1(nc.psum_tensor("ps_r", [32, P1T], F32))
                ps_lt = ent1(nc.psum_tensor("ps_lt", [128, 128], F32))
                ps_t1 = ent1(nc.psum_tensor("ps_t1", [32, 128], F32))
                ps_rp = ent1(nc.psum_tensor("ps_rp", [128, E], F32))
                ps_pm = ent1(nc.psum_tensor("ps_pm", [128, 256], F32))

                stat_done, xn_done, lsb_done, neg_done, gates_done = [], [], [], [], []
                for i in range(min(NP1, NP1_RUN)):
                    xri = xr[:].rearrange("p (s d) -> p s d", s=4)

                    @block.sync
                    def _(sy, i=i, xri=xri):
                        if i >= 1:
                            sy.wait_ge(sV.h, xn_done[i - 1])
                        sX.inc(sy.dma_start(
                            out=xri,
                            in_=x_own.rearrange("(i s p) d -> p i s d", s=4, p=128)[:, i]), 16)

                    @block.scalar
                    def _(sc, i=i, xri=xri):
                        sc.wait_ge(sX.h, 16 * (i + 1))
                        for s_ in range(4):
                            sc.activation(xn_tok[:, s_ * D:(s_ + 1) * D], xri[:, s_],
                                          AF.Square, accum_out=rc[:, s_:s_ + 1])
                        sfin(sc, sSc, op="square")
                    sq_done = sSc.n

                    @block.vector
                    def _(v, i=i, xri=xri, sq_done=sq_done):
                        v.wait_ge(sX.h, 16 * (i + 1))
                        v.wait_ge(sSc.h, sq_done)
                        for s_ in range(4):
                            v.reduce_sum(rc[:, 4 + s_:5 + s_], xri[:, s_], axis=AX.X)
                        for s_ in range(4):
                            v.tensor_scalar(rc[:, 4 + s_:5 + s_], rc[:, 4 + s_:5 + s_],
                                            1.0 / D, None, ALU.mult)
                        for s_ in range(4):
                            v.tensor_scalar(rc[:, s_:s_ + 1], rc[:, s_:s_ + 1],
                                            1.0 / D, None, ALU.mult)
                        for s_ in range(4):
                            v.tensor_scalar(rc[:, 20 + s_:21 + s_], rc[:, 4 + s_:5 + s_],
                                            rc[:, 4 + s_:5 + s_], None, ALU.mult)
                        for s_ in range(4):
                            v.tensor_scalar(rc[:, s_:s_ + 1], rc[:, s_:s_ + 1],
                                            rc[:, 20 + s_:21 + s_], None, ALU.subtract)
                        for s_ in range(4):
                            v.tensor_scalar(rc[:, s_:s_ + 1], rc[:, s_:s_ + 1],
                                            EPS, None, ALU.add)
                        for s_ in range(4):
                            v.reciprocal(rc[:, s_:s_ + 1], rc[:, s_:s_ + 1])
                        vfin(v, sV)
                    stat_done.append(sV.n)

                    @block.scalar
                    def _(sc, i=i, xri=xri):
                        sc.wait_ge(sV.h, stat_done[i])
                        for s_ in range(4):
                            sc.sqrt(rc[:, 8 + s_:9 + s_], rc[:, s_:s_ + 1])
                        for s_ in range(4):
                            sc.mul(rc[:, 12 + s_:13 + s_], rc[:, 4 + s_:5 + s_], -1.0)
                        for s_ in range(4):
                            sc.mul(rc[:, 12 + s_:13 + s_], rc[:, 12 + s_:13 + s_],
                                   rc[:, 8 + s_:9 + s_])
                        sfin(sc, sSc)
                    ist_done = sSc.n

                    @block.vector
                    def _(v, i=i, xri=xri, ist_done=ist_done):
                        v.wait_ge(sSc.h, ist_done)
                        if i >= 1:
                            v.wait_ge(sXN.h, 16 * i)
                        for s_ in range(4):
                            v.tensor_scalar(xn_tok[:, s_ * D:(s_ + 1) * D], xri[:, s_],
                                            rc[:, 8 + s_:9 + s_], rc[:, 12 + s_:13 + s_],
                                            ALU.mult, ALU.add)
                        vfin(v, sV)
                    xn_done.append(sV.n)

                    @block.sync
                    def _(sy, i=i):
                        sy.wait_ge(sV.h, xn_done[i])
                        sXN.inc(sy.dma_start(
                            out=xn_sc.ap().rearrange("(i s p) d -> p i s d", s=4, p=128)[:, i],
                            in_=xn_tok[:].rearrange("p (s d) -> p s d", s=4)), 16)
                        sy.wait_ge(sXN.h, sXN.n)
                        for d in range(NCH):
                            sXNF.inc(sy.dma_start(
                                out=xch(d, i * P1T, (i + 1) * P1T),
                                in_=xn_sc.ap()[i * P1T:(i + 1) * P1T, d * 128:(d + 1) * 128],
                                transpose=True), 16)
                    xnf_done = sXNF.n

                    @block.tensor
                    def _(t, i=i, xnf_done=xnf_done):
                        t.wait_ge(sW.h, w_consts)
                        t.wait_ge(sV.h, v_p0)
                        t.wait_ge(sXNF.h, xnf_done)
                        if i >= 1:
                            t.wait_ge(sV.h, lsb_done[i - 1])
                        for d in range(NCH):
                            ins = t.matmul(ps_r[:], wr_hi[:].rearrange("p (c e) -> p c e", c=NCH)[:, d],
                                           xch(d, i * P1T, (i + 1) * P1T),
                                           start=(d == 0), stop=(d == NCH - 1))
                        sT_.inc(ins)
                    rt_done = sT_.n

                    @block.vector
                    def _(v, i=i, rt_done=rt_done):
                        v.wait_ge(sT_.h, rt_done)
                        v.tensor_copy(lsb[:], ps_r[:])
                        if DEBUG and i == 0:
                            v.tensor_copy(wrtmp[:, 0:256] if False else lsb[:], ps_r[:])
                        vfin(v, sV)
                    lsb_done.append(sV.n)

                    if DEBUG and i == 0:
                        @block.sync
                        def _(sy):
                            sy.wait_ge(sV.h, lsb_done[0])
                            sOUT.inc(sy.dma_start(out=dbg_ps0, in_=lsb[:]), 16)

                    @block.tensor
                    def _(t, i=i):
                        t.wait_ge(sV.h, lsb_done[i])
                        if i >= 1:
                            t.wait_ge(sV.h, neg_done[i - 1])
                        for s_ in range(4):
                            ins = t.transpose(ps_lt[:, s_ * E:(s_ + 1) * E],
                                              lsb[:, s_ * 128:(s_ + 1) * 128],
                                              ident[0:32, 0:32])
                        sT_.inc(ins)
                    lt_done = sT_.n

                    @block.vector
                    def _(v, i=i, lt_done=lt_done):
                        v.wait_ge(sT_.h, lt_done)
                        if i >= 1:
                            v.wait_ge(sSc.h, exp_done[i - 1])
                        v.tensor_copy(lgt[:], ps_lt[:])
                        for s_ in range(4):
                            v.reduce_max(rc[:, 16 + s_:17 + s_], lgt[:, s_ * E:(s_ + 1) * E], axis=AX.X)
                        for s_ in range(4):
                            v.tensor_scalar(eqb[:, s_ * E:(s_ + 1) * E], lgt[:, s_ * E:(s_ + 1) * E],
                                            rc[:, 16 + s_:17 + s_], None, ALU.is_equal)
                        for s_ in range(4):
                            v.tensor_tensor(lge[:, s_ * E:(s_ + 1) * E],
                                            eqb[:, s_ * E:(s_ + 1) * E], c32[:], ALU.mult)
                        for s_ in range(4):
                            v.tensor_scalar(lge[:, s_ * E:(s_ + 1) * E],
                                            lge[:, s_ * E:(s_ + 1) * E], float(E), None, ALU.add)
                        for s_ in range(4):
                            v.tensor_reduce(top_cols[:, 4 * i + s_:4 * i + s_ + 1],
                                            lge[:, s_ * E:(s_ + 1) * E], axis=AX.X, op=ALU.min)
                        for s_ in range(4):
                            v.tensor_scalar(rc[:, 16 + s_:17 + s_], rc[:, 16 + s_:17 + s_],
                                            -1.0, None, ALU.mult)
                        vfin(v, sV)
                    neg_done.append(sV.n)

                    @block.scalar
                    def _(sc, i=i):
                        sc.wait_ge(sV.h, neg_done[i])
                        for s_ in range(4):
                            sc.activation(lge[:, s_ * E:(s_ + 1) * E],
                                          lgt[:, s_ * E:(s_ + 1) * E],
                                          AF.Exp, bias=rc[:, 16 + s_:17 + s_], scale=1.0)
                        sfin(sc, sSc)
                    if i == 0:
                        exp_done = []
                    exp_done.append(sSc.n)

                    @block.vector
                    def _(v, i=i):
                        v.wait_ge(sSc.h, exp_done[i])
                        for s_ in range(4):
                            v.reduce_sum(rc[:, 20 + s_:21 + s_], lge[:, s_ * E:(s_ + 1) * E], axis=AX.X)
                        for s_ in range(4):
                            v.reciprocal(gate_cols[:, 4 * i + s_:4 * i + s_ + 1],
                                         rc[:, 20 + s_:21 + s_])
                        vfin(v, sV)
                    gates_done.append(sV.n)

                @G(bool(gates_done), block.tensor)
                def _(t):
                    t.wait_ge(sV.h, gates_done[-1])
                    sT_.inc(t.transpose(ps_t1[:], top_cols[:], ident[:]))
                t1_done = sT_.n

                @block.vector
                def _(v):
                    v.wait_ge(sT_.h, t1_done)
                    v.tensor_copy(t1t[:], ps_t1[:])
                    vfin(v, sV)
                t1t_done = sV.n

                @block.gpsimd
                def _(g):
                    g.wait_ge(sV.h, t1t_done)
                    sG.inc(g.dma_start(out=ag1_in.ap().rearrange("(j p) one -> j (p one)", p=128),
                                       in_=t1t[:]), 16)
                    g.wait_ge(sG.h, sG.n)
                    sCC.inc(g.collective_compute(
                        "AllGather", ALU.bypass, replica_groups=[list(range(NCORE))],
                        ins=[ag1_in.ap()], outs=[ag1_out.ap()]))
                    g.wait_ge(sCC.h, sCC.n)
                    sG.inc(g.dma_start(out=t1_all[:],
                                       in_=ag1_out.rearrange("(p c) one -> p (c one)", p=128)), 16)
                t1all_done = sG.n

                @G(STAGE >= 12, block.vector)
                def _(v):
                    v.wait_ge(sG.h, t1all_done)
                    v.memset(ones_t[:], 1.0)
                    v.memset(rank[:], 0.0)
                    for e in range(E):
                        v.tensor_scalar(ohb[:], t1_all[:], float(e), None, ALU.is_equal)
                        v.tensor_tensor_scan(scn[:], ones_t[:], ohb[:], 0.0, ALU.mult, ALU.add)
                        v.tensor_tensor(ohb[:], scn[:], ohb[:], ALU.mult)
                        v.tensor_add(rank[:], rank[:], ohb[:])
                        v.tensor_copy(tot[:, e:e + 1], scn[:, 255:256])
                    v.tensor_scalar(jkv[:, 1:2], jkv[:, 1:2], 1.0, None, ALU.mult)
                    v.tensor_scalar(jkv[:, 2:3], jkv[:, 2:3], 1.0, None, ALU.mult)
                    v.tensor_copy(tot_bf[:], tot[:])
                    vfin(v, sV)
                tot_done = sV.n

                @G(STAGE >= 12, block.tensor)
                def _(t):
                    t.wait_ge(sV.h, tot_done)
                    sT_.inc(t.matmul(ps_rp[:], ltu[:], tot_bf[:], start=True, stop=True))
                rp_done = sT_.n

                @G(STAGE >= 12, block.vector)
                def _(v):
                    v.wait_ge(sT_.h, rp_done)
                    v.tensor_copy(rp_sb[:], ps_rp[:])
                    v.tensor_scalar(jkv[:, 1:2], jkv[:, 1:2], 1.0, None, ALU.mult)
                    for e in range(E):
                        v.tensor_scalar(ohb[:], t1_all[:], float(e), None, ALU.is_equal)
                        v.tensor_scalar(ohb[:], ohb[:], rp_sb[:, e:e + 1], None, ALU.mult)
                        v.tensor_add(rank[:], rank[:], ohb[:])
                    v.tensor_scalar(pos_t[:], rank[:], -1.0, None, ALU.add)
                    v.tensor_scalar(slot_t[:], t1_all[:], float(CAP), None, ALU.mult)
                    v.tensor_add(slot_t[:], slot_t[:], pos_t[:])
                    v.tensor_scalar(ohb[:], pos_t[:], float(CAP - 1), None, ALU.min)
                    v.tensor_tensor(ohb[:], ohb[:], pos_t[:], ALU.is_equal)
                    v.tensor_scalar(slot_t[:], slot_t[:], -DUMP, None, ALU.add)
                    v.tensor_tensor(slot_t[:], slot_t[:], ohb[:], ALU.mult)
                    v.tensor_scalar(slot_t[:], slot_t[:], DUMP, None, ALU.add)
                    vfin(v, sV)
                slot_done = sV.n

                @G(STAGE >= 13, block.tensor)
                def _(t):
                    t.wait_ge(sV.h, slot_done)
                    sT_.inc(t.matmul(ps_pm[:], permk[:], slot_t[:], start=True, stop=True))
                pm_done = sT_.n

                @G(STAGE >= 13, block.vector)
                def _(v):
                    v.wait_ge(sT_.h, pm_done)
                    v.tensor_copy(lsb[0:16, 0:256], ps_pm[0:16, :])
                    vfin(v, sV)
                sm_done = sV.n

                @G(STAGE >= 13, block.sync)
                def _(sy):
                    sy.wait_ge(sV.h, sm_done)
                    sS2.inc(sy.dma_start(out=smine_h.ap().rearrange("(r c) one -> r (c one)", r=16),
                                         in_=lsb[0:16, 0:256]), 16)
                    sy.wait_ge(sS2.h, sS2.n)
                    sS2.inc(sy.dma_start(out=slm[:],
                                         in_=smine_h.ap().rearrange("(p c) one -> p (c one)", p=128)), 16)
                    sS2.inc(sy.dma_start(out=smT[:],
                                         in_=smine_h.ap().rearrange("(j p) one -> j (p one)", p=128)), 16)
                sm_ld = sS2.n

                @G(STAGE >= 13, block.vector)
                def _(v):
                    v.wait_ge(sS2.h, sm_ld)
                    v.tensor_scalar(slm[:], slm[:], float(TBL - 1), 0.0, ALU.min, ALU.max)
                    v.tensor_scalar(jkv[:, 1:2], jkv[:, 1:2], 1.0, None, ALU.mult)
                    v.tensor_copy(slm_i[:], slm[:])
                    vfin(v, sV)
                slmi_done = sV.n

                @G(STAGE >= 13, block.tensor)
                def _(t):
                    t.wait_ge(sS2.h, sm_ld)
                    sT_.inc(t.transpose(ps_pm[:, 0:32], smT[:], ident[0:32, 0:32]))
                smt_done = sT_.n

                @G(STAGE >= 13, block.vector)
                def _(v):
                    v.wait_ge(sT_.h, smt_done)
                    v.tensor_copy(slotf[:], ps_pm[:, 0:32])
                    v.tensor_scalar(jkv[:, 1:2], jkv[:, 1:2], 1.0, None, ALU.mult)
                    v.tensor_scalar(gmask[:], slotf[:], DUMP, None, ALU.is_equal)
                    v.tensor_scalar(slotf[:], slotf[:], DUMP - 1.0, 0.0, ALU.min, ALU.max)
                    v.tensor_scalar(gmask[:], gmask[:], -1.0, 1.0, ALU.mult, ALU.add)
                    v.tensor_copy(slotT_i[:], slotf[:])
                    v.tensor_tensor(gmask[:], gmask[:], gate_cols[:], ALU.mult)
                    vfin(v, sV)
                p5idx_done = sV.n

                @G(STAGE >= 14, block.gpsimd)
                def _(g):
                    g.wait_ge(sV.h, slmi_done)
                    g.wait_ge(sS2.h, s2_sent)
                    for cc in range(32):
                        sG.inc(g.indirect_dma_start(
                            out=ptab.ap(),
                            out_offset=bass.IndirectOffsetOnAxis(ap=slm_i[:, cc:cc + 1], axis=0),
                            in_=ids_f[:, cc:cc + 1], in_offset=None), 16)
                    g.wait_ge(sG.h, sG.n)
                    sCC.inc(g.collective_compute(
                        "AllGather", ALU.bypass, replica_groups=[list(range(NCORE))],
                        ins=[ptab.ap()], outs=[ag2_out.ap()]))
                ag2_done = sCC.n

                pb_marks = []
                mn_marks = []
                for r in range(NCORE):
                    @G(STAGE >= 14, block.sync)
                    def _(sy, r=r):
                        if r == 0:
                            sy.wait_ge(sCC.h, ag2_done)
                        if r >= 2:
                            sy.wait_ge(sV.h, mn_marks[r - 2])
                        sS2.inc(sy.dma_start(
                            out=pblk[:, (r % 2) * 321:(r % 2 + 1) * 321],
                            in_=ag2_out.rearrange("(r p c) one -> r p (c one)", p=128, c=321)[r]), 16)
                    pb_marks.append(sS2.n)

                    @G(STAGE >= 14, block.vector)
                    def _(v, r=r):
                        v.wait_ge(sS2.h, pb_marks[r])
                        if r == 0:
                            v.tensor_scalar(sent_t[:], pblk[:, 0:321], 1.0, None, ALU.mult)
                        else:
                            v.tensor_tensor(sent_t[:], sent_t[:],
                                            pblk[:, (r % 2) * 321:(r % 2 + 1) * 321], ALU.min)
                        vfin(v, sV)
                    mn_marks.append(sV.n)
                idl_done = sV.n

                @G(STAGE >= 14, block.sync)
                def _(sy):
                    sy.wait_ge(sV.h, idl_done)
                    sS2.inc(sy.dma_start(out=idl_hb.ap().rearrange("(p c) one -> p (c one)", p=128),
                                         in_=sent_t[:]), 16)
                idlh_done = sS2.n

                @G(STAGE >= 15, block.gpsimd)
                def _(g):
                    g.wait_ge(sS2.h, idlh_done)
                    sG.inc(g.indirect_dma_start(
                        out=wint[:], out_offset=None,
                        in_=idl_hb.rearrange("(a b) one -> a (b one)", b=128),
                        in_offset=bass.IndirectOffsetOnAxis(ap=widx[:], axis=0)), 16)
                win_g = sG.n


                @G(STAGE >= 15, block.sync)
                def _(sy):
                    sy.wait_ge(sG.h, win_g)
                    sS2.inc(sy.dma_start(out=win_hb.ap().rearrange("(j p) one -> j (p one)", p=128),
                                         in_=wint[:]), 16)
                    sy.wait_ge(sS2.h, sS2.n)
                    sS2.inc(sy.dma_start(out=winr_f[:],
                                         in_=win_hb.rearrange("(p c) one -> p (c one)", p=128)), 16)
                winr_ld = sS2.n

                @G(STAGE >= 15, block.vector)
                def _(v):
                    v.wait_ge(sS2.h, winr_ld)
                    v.tensor_scalar(winr_f[:], winr_f[:], float(T - 1), 0.0, ALU.min, ALU.max)
                    v.tensor_scalar(jkv[:, 1:2], jkv[:, 1:2], 1.0, None, ALU.mult)
                    v.tensor_copy(winr_i[:], winr_f[:])
                    vfin(v, sV)
                win_done = sV.n

                if DEBUG:
                    @block.sync
                    def _(sy):
                        sy.wait_ge(sS2.h, winr_ld)
                        sOUT.inc(sy.dma_start(out=dbg_win, in_=winr_f[:]), 16)
                        sy.wait_ge(sV.h, p5idx_done)
                        sOUT.inc(sy.dma_start(out=dbg_gate, in_=gate_cols[:]), 16)
                        sOUT.inc(sy.dma_start(out=dbg_gm, in_=gmask[:]), 16)


                    @block.gpsimd
                    def _(g):
                        g.wait_ge(sG.h, t1all_done)
                        sOUT.inc(g.dma_start(out=dbg_t1, in_=t1_all[:]), 16)
                        g.wait_ge(sS2.h, sm_ld)
                        sOUT.inc(g.dma_start(out=dbg_sm, in_=smine_h.ap()), 16)
                        g.wait_ge(sS2.h, idlh_done)
                        sOUT.inc(g.dma_start(out=dbg_idl, in_=idl_hb.ap()), 16)


            g3_done, st3_done = [], []
            ln3_v, ln3_s = [], []

            @block.gpsimd
            def _(g):
                g.wait_ge(sV.h, win_done)
                for c in (range(40) if STAGE >= 20 else []):
                    if c >= 2:
                        g.wait_ge(sST3.h, 16 * (c - 1))
                    sG3.inc(g.indirect_dma_start(
                        out=xgb[:, (c % 2) * D:(c % 2 + 1) * D], out_offset=None,
                        in_=x_all,
                        in_offset=bass.IndirectOffsetOnAxis(ap=winr_i[:, c:c + 1], axis=0)), 16)
                    g3_done.append(sG3.n)

            def emit_ln3(c):
                @block.scalar
                def _(sc, c=c):
                    sc.wait_ge(sG3.h, 16 * (c + 1))
                    sc.activation(xnc_b[:, (c % 2) * D:(c % 2 + 1) * D],
                                  xgb[:, (c % 2) * D:(c % 2 + 1) * D],
                                  AF.Square, accum_out=st3[:, 0:1])
                    sfin(sc, sSc, op="square")
                ln3_s.append(sSc.n)

                @block.vector
                def _(v, c=c):
                    v.wait_ge(sSc.h, ln3_s[-1])
                    v.reduce_sum(st3[:, 1:2], xgb[:, (c % 2) * D:(c % 2 + 1) * D], axis=AX.X)
                    v.tensor_scalar(st3[:, 0:1], st3[:, 0:1], 1.0 / D, None, ALU.mult)
                    v.tensor_scalar(st3[:, 1:2], st3[:, 1:2], 1.0 / D, None, ALU.mult)
                    v.tensor_scalar(jkv[:, 1:2], jkv[:, 1:2], 1.0, None, ALU.mult)
                    v.tensor_scalar(st3[:, 2:3], st3[:, 1:2], st3[:, 1:2], None, ALU.mult)
                    v.tensor_scalar(jkv[:, 1:2], jkv[:, 1:2], 1.0, None, ALU.mult)
                    v.tensor_scalar(st3[:, 0:1], st3[:, 0:1], st3[:, 2:3], None, ALU.subtract)
                    v.tensor_scalar(jkv[:, 1:2], jkv[:, 1:2], 1.0, None, ALU.mult)
                    v.tensor_scalar(st3[:, 0:1], st3[:, 0:1], EPS, None, ALU.add)
                    v.tensor_scalar(jkv[:, 1:2], jkv[:, 1:2], 1.0, None, ALU.mult)
                    v.reciprocal(st3[:, 0:1], st3[:, 0:1])
                    vfin(v, sV)
                ln3_v.append(sV.n)

                @block.scalar
                def _(sc, c=c):
                    sc.wait_ge(sV.h, ln3_v[-1])
                    sc.sqrt(st3[:, 3:4], st3[:, 0:1])
                    sc.mul(st3[:, 4:5], st3[:, 1:2], -1.0)
                    sc.mul(jks[:, 1:2], jks[:, 1:2], 1.0)
                    sc.mul(st3[:, 4:5], st3[:, 4:5], st3[:, 3:4])
                    sfin(sc, sSc)
                ln3_s.append(sSc.n)

                @block.vector
                def _(v, c=c):
                    v.wait_ge(sSc.h, ln3_s[-1])
                    if c >= 2:
                        v.wait_ge(sST3.h, 16 * (c - 1))
                    v.tensor_scalar(xnc_b[:, (c % 2) * D:(c % 2 + 1) * D],
                                    xgb[:, (c % 2) * D:(c % 2 + 1) * D],
                                    st3[:, 3:4], st3[:, 4:5], ALU.mult, ALU.add)
                    vfin(v, sV)
                ln3_v.append(sV.n)

                @block.sync
                def _(sy, c=c):
                    sy.wait_ge(sV.h, ln3_v[-1])
                    sST3.inc(sy.dma_start(
                        out=xnc_hb.ap().rearrange("(p cc) d -> p cc d", cc=40)[:, c],
                        in_=xnc_b[:, (c % 2) * D:(c % 2 + 1) * D]), 16)
                st3_done.append(sST3.n)

            with ExitStack() as ctx4:
                ent4 = ctx4.enter_context
                q_f = ent4(nc.sbuf_tensor("q_f", [128, NCH * P4T], F32))
                v_f = ent4(nc.sbuf_tensor("v_f", [128, NCH * P4T], F32))
                st_f = ent4(nc.sbuf_tensor("st_f", [128, NCH * P4T], F32))
                ybf = ent4(nc.sbuf_tensor("ybf", [128, NCH * P4T], BF16))
                ysq = ent4(nc.sbuf_tensor("ysq", [128, NCH * P4T], BF16))
                gs_b = ent4(nc.sbuf_tensor("gs_b", [128, 2 * NCH * P4T], BF16))
                mrow = ent4(nc.sbuf_tensor("mrow", [1, P4T], F32))
                isrow = ent4(nc.sbuf_tensor("isrow", [1, P4T], F32))
                s_b = ent4(nc.sbuf_tensor("s_b", [128, 2 * D], BF16))
                ps_p0 = ent4(nc.psum_tensor("ps_p0", [128, P4T], F32))
                ps_p1 = ent4(nc.psum_tensor("ps_p1", [128, P4T], F32))
                ps_s1 = ent4(nc.psum_tensor("ps_s1", [1, P4T], F32))
                ps_s2 = ent4(nc.psum_tensor("ps_s2", [1, P4T], F32))
                ps_mu = ent4(nc.psum_tensor("ps_mu", [128, P4T], F32))
                ps_is = ent4(nc.psum_tensor("ps_is", [128, P4T], F32))
                ps_wa = ent4(nc.psum_tensor("ps_wa", [128, 512], F32))
                ps_wb = ent4(nc.psum_tensor("ps_wb", [128, 512], F32))

                s4_marks = []
                sev_marks = []
                nln = 40 if STAGE >= 20 else 0
                ln3_ptr = 0
                GHALF = NCH * P4T
                pe_chain = []
                y_l, ysq_l, smm_l, mv_l, is_l, bc_l, yg_l, g_l = {}, {}, {}, {}, {}, {}, {}, {}

                def emit_epi1(ip):
                    @block.tensor
                    def _(t, ip=ip):
                        t.wait_ge(sV.h, y_l[ip])
                        t.wait_ge(sSc.h, ysq_l[ip])
                        for d in range(NCH):
                            t.matmul(ps_s1[:], onc[:], ybf[:, d * P4T:(d + 1) * P4T],
                                     start=(d == 0), stop=(d == NCH - 1))
                        for d in range(NCH):
                            ins = t.matmul(ps_s2[:], onc[:], ysq[:, d * P4T:(d + 1) * P4T],
                                           start=(d == 0), stop=(d == NCH - 1))
                        sT_.inc(ins)
                    smm_l[ip] = sT_.n

                    @block.vector
                    def _(v, ip=ip):
                        v.wait_ge(sT_.h, smm_l[ip])
                        v.tensor_scalar(mrow[:], ps_s1[:], 1.0 / D, None, ALU.mult)
                        v.tensor_scalar(isrow[:], ps_s2[:], 1.0 / D, None, ALU.mult)
                        v.tensor_tensor(ps_s1[:], mrow[:], mrow[:], ALU.mult)
                        v.tensor_scalar(jkv[:, 1:2], jkv[:, 1:2], 1.0, None, ALU.mult)
                        v.tensor_tensor(isrow[:], isrow[:], ps_s1[:], ALU.subtract)
                        v.tensor_scalar(isrow[:], isrow[:], EPS, None, ALU.add)
                        v.reciprocal(isrow[:], isrow[:])
                        vfin(v, sV)
                    mv_l[ip] = sV.n

                    @block.scalar
                    def _(sc, ip=ip):
                        sc.wait_ge(sV.h, mv_l[ip])
                        sc.sqrt(isrow[:], isrow[:])
                        sfin(sc, sSc)
                    is_l[ip] = sSc.n

                def emit_epi2(ip):
                    @block.tensor
                    def _(t, ip=ip):
                        t.wait_ge(sSc.h, is_l[ip])
                        t.matmul(ps_mu[:], onr[:], mrow[:], start=True, stop=True)
                        sT_.inc(t.matmul(ps_is[:], onr[:], isrow[:], start=True, stop=True))
                    bc_l[ip] = sT_.n

                    @block.vector
                    def _(v, ip=ip):
                        v.wait_ge(sT_.h, bc_l[ip])
                        v.wait_ge(sSc.h, g_l[ip])
                        for d in range(NCH):
                            v.tensor_copy(st_f[:, d * P4T:(d + 1) * P4T],
                                          ybf[:, d * P4T:(d + 1) * P4T])
                        for d in range(NCH):
                            v.tensor_tensor(st_f[:, d * P4T:(d + 1) * P4T],
                                            st_f[:, d * P4T:(d + 1) * P4T], ps_mu[:], ALU.subtract)
                        for d in range(NCH):
                            v.tensor_tensor(st_f[:, d * P4T:(d + 1) * P4T],
                                            st_f[:, d * P4T:(d + 1) * P4T], ps_is[:], ALU.mult)
                        for d in range(NCH):
                            v.tensor_scalar(ybf[:, d * P4T:(d + 1) * P4T],
                                            st_f[:, d * P4T:(d + 1) * P4T],
                                            gam_c[:, d:d + 1], bet_c[:, d:d + 1],
                                            ALU.mult, ALU.add)
                        v.tensor_tensor(ysq[:], ybf[:],
                                        gs_b[:, (ip % 2) * GHALF:(ip % 2 + 1) * GHALF], ALU.mult)
                        vfin(v, sV)
                    yg_l[ip] = sV.n

                def emit_wo(ip):
                    for sub in range(2):
                        @block.tensor
                        def _(t, ip=ip, sub=sub):
                            t.wait_ge(sV.h, yg_l[ip])
                            if sev_marks:
                                t.wait_ge(sSc.h, sev_marks[-1])
                            for d in range(NCH):
                                t.matmul(ps_wa[:], ysq[:, d * P4T + sub * 128:d * P4T + (sub + 1) * 128],
                                         wch(wo_sb, d)[:, 0:512], start=(d == 0), stop=(d == NCH - 1))
                            for d in range(NCH):
                                ins = t.matmul(ps_wb[:], ysq[:, d * P4T + sub * 128:d * P4T + (sub + 1) * 128],
                                               wch(wo_sb, d)[:, 512:1024], start=(d == 0), stop=(d == NCH - 1))
                            sT_.inc(ins)
                        wo_mm = sT_.n

                        @block.scalar
                        def _(sc, ip=ip, sub=sub, wo_mm=wo_mm):
                            sc.wait_ge(sT_.h, wo_mm)
                            idx = 2 * (ip - 1) + sub
                            if idx >= 0:
                                sc.wait_ge(sS4.h, s4_marks[idx])
                            sc.copy(s_b[:, sub * D:sub * D + 512], ps_wa[:])
                            sc.copy(s_b[:, sub * D + 512:(sub + 1) * D], ps_wb[:])
                            sfin(sc, sSc)
                        sev_marks.append(sSc.n)

                        @block.sync
                        def _(sy, ip=ip, sub=sub):
                            sy.wait_ge(sSc.h, sev_marks[-1])
                            sS4.inc(sy.dma_start(
                                out=s_sc.ap()[ip * P4T + sub * 128:ip * P4T + (sub + 1) * 128, :],
                                in_=s_b[:, sub * D:(sub + 1) * D]), 16)
                        s4_marks.append(sS4.n)

                NTOT = min(NP4, NP4_RUN) if STAGE >= 30 else 0

                def emit_proj(i, pj):
                    lo, hi = i * P4T, (i + 1) * P4T
                    wsb, dst, eng, act = ((wq_sb, q_f, "v", None),
                                          (wv_sb, v_f, "v", None),
                                          (wgt_sb, None, "s", AF.Silu))[pj]
                    if True:
                        for e in range(NCH):
                            psq = ps_p0 if e % 2 == 0 else ps_p1
                            gidx = len(pe_chain)

                            @block.tensor
                            def _(t, i=i, e=e, psq=psq, wsb=wsb, gidx=gidx):
                                if gidx == 0:
                                    t.wait_ge(sW.h, w_all)
                                if gidx >= 2:
                                    kind, mk = pe_chain[gidx - 2]
                                    t.wait_ge(sV.h if kind == "v" else sSc.h, mk)
                                for d in range(NCH):
                                    ins = t.matmul(psq[:], wch(wsb, d)[:, e * 128:(e + 1) * 128],
                                                   xch(d, lo, hi), start=(d == 0), stop=(d == NCH - 1))
                                sT_.inc(ins)
                            mmk = sT_.n

                            if eng == "v":
                                @block.vector
                                def _(v, i=i, e=e, pj=pj, psq=psq, dst=dst, mmk=mmk):
                                    v.wait_ge(sT_.h, mmk)
                                    if pj == 1 and e == 0 and i >= 1:
                                        v.wait_ge(sSc.h, ysq_l[i - 1])
                                    v.tensor_copy(dst[:, e * P4T:(e + 1) * P4T], psq[:])
                                    vfin(v, sV)
                                pe_chain.append(("v", sV.n))
                            else:
                                @block.scalar
                                def _(sc, i=i, e=e, psq=psq, mmk=mmk, act=act):
                                    sc.wait_ge(sT_.h, mmk)
                                    if e == 0 and i >= 2:
                                        sc.wait_ge(sV.h, yg_l[i - 2])
                                    sc.activation(gs_b[:, (i % 2) * GHALF + e * P4T:
                                                       (i % 2) * GHALF + (e + 1) * P4T], psq[:], act)
                                    sfin(sc, sSc, op="silu")
                                pe_chain.append(("s", sSc.n))
                        if pj == 2:
                            g_l[i] = sSc.n

                for i in range(NTOT):
                    emit_proj(i, 0)
                    if i >= 1:
                        emit_epi1(i - 1)
                    emit_proj(i, 1)
                    if i >= 1:
                        emit_epi2(i - 1)
                    emit_proj(i, 2)

                    @block.vector
                    def _(v, i=i):
                        for d in range(NCH):
                            init = 0.0 if i % (NP4 // 2) == 0 else carry[:, d:d + 1]
                            v.tensor_tensor_scan(st_f[:, d * P4T:(d + 1) * P4T],
                                                 lam_bt[:, d * P4T:(d + 1) * P4T],
                                                 v_f[:, d * P4T:(d + 1) * P4T],
                                                 init, ALU.mult, ALU.add)
                        for d in range(NCH):
                            v.tensor_copy(carry[:, d:d + 1], st_f[:, d * P4T + P4T - 1:d * P4T + P4T])
                        v.tensor_tensor(v_f[:], q_f[:], st_f[:], ALU.mult)
                        v.tensor_copy(ybf[:], v_f[:])
                        vfin(v, sV)
                    y_l[i] = sV.n

                    if i >= 1:
                        emit_wo(i - 1)

                    @block.scalar
                    def _(sc, i=i):
                        sc.wait_ge(sV.h, y_l[i])
                        sc.activation(ysq[:], v_f[:], AF.Square)
                        sfin(sc, sSc, op="square")
                    ysq_l[i] = sSc.n

                    for _k in range(2):
                        if ln3_ptr < nln:
                            emit_ln3(ln3_ptr)
                            ln3_ptr += 1
                if NTOT >= 1:
                    emit_epi1(NTOT - 1)
                    emit_epi2(NTOT - 1)
                    emit_wo(NTOT - 1)
                while ln3_ptr < nln:
                    emit_ln3(ln3_ptr)
                    ln3_ptr += 1

                if DEBUG:
                    @block.gpsimd
                    def _(g):
                        if s4_marks:
                            g.wait_ge(sS4.h, s4_marks[-1])
                            sOUT.inc(g.dma_start(out=dbg_s, in_=s_sc.ap()), 16)
                            if NP4_RUN == 1:
                                sOUT.inc(g.dma_start(out=dbg_q, in_=q_f[:]), 16)
                                sOUT.inc(g.dma_start(out=dbg_v, in_=v_f[:]), 16)
                                sOUT.inc(g.dma_start(out=dbg_st, in_=st_f[:]), 16)
                                sOUT.inc(g.dma_start(out=dbg_yg, in_=ysq[:]), 16)
                        g.wait_ge(sXNF.h, sXNF.n)
                        sOUT.inc(g.dma_start(out=dbg_xnfm, in_=xn_fm[:, 3 * NT:4 * NT]), 16)
                        sOUT.inc(g.dma_start(out=dbg_xnsc, in_=xn_sc.ap()), 16)
                        if st3_done:
                            g.wait_ge(sST3.h, st3_done[-1])
                            sOUT.inc(g.dma_start(out=dbg_xnc, in_=xnc_hb.ap()), 16)


        with ExitStack() as ctx3:
            ent3 = ctx3.enter_context
            bufT = ent3(nc.sbuf_tensor("bufT", [128, 2 * NCH * CAP], BF16))
            wg_e = ent3(nc.sbuf_tensor("wg_e", [128, 2 * NCH * D], BF16))
            wu_e = ent3(nc.sbuf_tensor("wu_e", [128, 2 * NCH * D], BF16))
            wd_e = ent3(nc.sbuf_tensor("wd_e", [128, NCH * D], BF16))
            mid = ent3(nc.sbuf_tensor("mid", [128, NCH * CAP], BF16))
            sg_t = ent3(nc.sbuf_tensor("sg_t", [128, 512], BF16))
            ub_t = ent3(nc.sbuf_tensor("ub_t", [128, 512], BF16))
            yb3 = ent3(nc.sbuf_tensor("yb3", [128, D], BF16))
            ps_g = ent3(nc.psum_tensor("ps_g", [128, 512], F32))
            ps_u = ent3(nc.psum_tensor("ps_u", [128, 512], F32))
            ps_y0 = ent3(nc.psum_tensor("ps_y0", [128, 512], F32))
            ps_y1 = ent3(nc.psum_tensor("ps_y1", [128, 512], F32))

            WIN = ((0, 512), (512, 1024), (1024, 1280))
            gu_done, y3_done, td_marks = [], [], []
            we_marks = []
            for j in (range(NE) if STAGE >= 40 else []):
                @block.sync
                def _(sy, j=j):
                    sy.wait_ge(sST3.h, 16 * 40)
                    if j == 0 and s4_marks:
                        sy.wait_ge(sS4.h, s4_marks[-1])
                    if j >= 2:
                        sy.wait_ge(sT_.h, gu_done[j - 2])
                    for h in range(NCH):
                        sTD.inc(sy.dma_start(
                            out=bufT[:, (j % 2) * NCH * CAP + h * CAP:(j % 2) * NCH * CAP + (h + 1) * CAP],
                            in_=xnc_hb.ap()[j * CAP:(j + 1) * CAP, h * 128:(h + 1) * 128],
                            transpose=True), 16)
                td_marks.append(sTD.n)

                @block.gpsimd
                def _(g, j=j):
                    if j == 0 and s4_marks:
                        g.wait_ge(sS4.h, s4_marks[-1])
                    if j >= 2:
                        g.wait_ge(sT_.h, gu_done[j - 2])
                    sWE.inc(g.dma_start(out=wg_e[:].rearrange("p (b c d) -> p b c d", b=2, c=NCH)[:, j % 2],
                                        in_=weg.rearrange("j (c p) d -> j p c d", p=128)[j]), 16)
                    sWE.inc(g.dma_start(out=wu_e[:].rearrange("p (b c d) -> p b c d", b=2, c=NCH)[:, j % 2],
                                        in_=weu.rearrange("j (c p) d -> j p c d", p=128)[j]), 16)
                    if j >= 1:
                        g.wait_ge(sT_.h, y3_done[j - 1])
                    sWE.inc(g.dma_start(out=wd_e[:].rearrange("p (c d) -> p c d", c=NCH),
                                        in_=wed.rearrange("j (c p) d -> j p c d", p=128)[j]), 16)
                we_marks.append(sWE.n)

                bT = lambda h, lo, hi, j=j: bufT[:, (j % 2) * NCH * CAP + h * CAP + lo:
                                                 (j % 2) * NCH * CAP + h * CAP + hi]
                ev3_marks = []
                for (wlo, whi) in WIN:
                    for f in range(NCH):
                        @block.tensor
                        def _(t, j=j, f=f, wlo=wlo, whi=whi):
                            t.wait_ge(sTD.h, td_marks[j])
                            t.wait_ge(sWE.h, we_marks[j] - 16)
                            if ev3_marks:
                                t.wait_ge(sV.h, ev3_marks[-1])
                            for h in range(NCH):
                                t.matmul(ps_g[:, 0:whi - wlo],
                                         wg_e[:].rearrange("p (b c d) -> p b c d", b=2, c=NCH)[:, j % 2, h,
                                                                                               f * 128:(f + 1) * 128]
                                         if False else
                                         wg_e[:, (j % 2) * NCH * D + h * D + f * 128:
                                              (j % 2) * NCH * D + h * D + (f + 1) * 128],
                                         bT(h, wlo, whi), start=(h == 0), stop=(h == NCH - 1))
                            for h in range(NCH):
                                ins = t.matmul(ps_u[:, 0:whi - wlo],
                                               wu_e[:, (j % 2) * NCH * D + h * D + f * 128:
                                                    (j % 2) * NCH * D + h * D + (f + 1) * 128],
                                               bT(h, wlo, whi), start=(h == 0), stop=(h == NCH - 1))
                            sT_.inc(ins)
                        gu_mm = sT_.n

                        @block.scalar
                        def _(sc, gu_mm=gu_mm, wlo=wlo, whi=whi):
                            sc.wait_ge(sT_.h, gu_mm)
                            sc.activation(sg_t[:, 0:whi - wlo], ps_g[:, 0:whi - wlo], AF.Silu)
                            sfin(sc, sSc, op="silu")
                        sg_done = sSc.n

                        @block.vector
                        def _(v, f=f, wlo=wlo, whi=whi, gu_mm=gu_mm, sg_done=sg_done):
                            v.wait_ge(sT_.h, gu_mm)
                            v.wait_ge(sSc.h, sg_done)
                            v.tensor_copy(ub_t[:, 0:whi - wlo], ps_u[:, 0:whi - wlo])
                            v.tensor_scalar(jkv[:, 1:2], jkv[:, 1:2], 1.0, None, ALU.mult)
                            v.tensor_tensor(mid[:, f * CAP + wlo:f * CAP + whi],
                                            sg_t[:, 0:whi - wlo], ub_t[:, 0:whi - wlo], ALU.mult)
                            vfin(v, sV)
                        ev3_marks.append(sV.n)
                gu_done.append(sT_.n)

                yev_marks = []
                for sc_ in range(10):
                    @block.tensor
                    def _(t, j=j, sc_=sc_):
                        t.wait_ge(sV.h, ev3_marks[-1])
                        t.wait_ge(sWE.h, we_marks[j])
                        if yev_marks:
                            t.wait_ge(sV.h, yev_marks[-1])
                        for f in range(NCH):
                            t.matmul(ps_y0[:], mid[:, f * CAP + sc_ * 128:f * CAP + (sc_ + 1) * 128],
                                     wd_e[:, f * D:f * D + 512], start=(f == 0), stop=(f == NCH - 1))
                        for f in range(NCH):
                            ins = t.matmul(ps_y1[:], mid[:, f * CAP + sc_ * 128:f * CAP + (sc_ + 1) * 128],
                                           wd_e[:, f * D + 512:(f + 1) * D], start=(f == 0), stop=(f == NCH - 1))
                        sT_.inc(ins)
                    ymm = sT_.n

                    @block.vector
                    def _(v, ymm=ymm):
                        v.wait_ge(sT_.h, ymm)
                        v.tensor_copy(yb3[:, 0:512], ps_y0[:])
                        v.tensor_copy(yb3[:, 512:1024], ps_y1[:])
                        vfin(v, sV)
                    yev_marks.append(sV.n)

                    @block.sync
                    def _(sy, j=j, sc_=sc_):
                        sy.wait_ge(sV.h, yev_marks[-1])
                        sY3.inc(sy.dma_start(
                            out=ag3_in.ap()[j * CAP + sc_ * 128:j * CAP + (sc_ + 1) * 128, :],
                            in_=yb3[:]), 16)
                y3_done.append(sT_.n)

            if STAGE >= 40:
                @block.gpsimd
                def _(g):
                    g.wait_ge(sY3.h, sY3.n)
                    sCC.inc(g.collective_compute(
                        "AllGather", ALU.bypass, replica_groups=[list(range(NCORE))],
                        ins=[ag3_in.ap()], outs=[ag3_out.ap()]))
            ag3_done = sCC.n

            if DEBUG and STAGE >= 40:
                @block.gpsimd
                def _(g):
                    g.wait_ge(sY3.h, sY3.n)
                    sOUT.inc(g.dma_start(out=dbg_y3, in_=ag3_in.ap()), 16)

        with ExitStack() as ctx5:
            ent5 = ctx5.enter_context
            PD = 3
            mg = ent5(nc.sbuf_tensor("mg", [128, PD * D], BF16))
            xrow = ent5(nc.sbuf_tensor("xrow", [128, PD * D], F32))
            xnr = ent5(nc.sbuf_tensor("xnr", [128, PD * D], BF16))
            srow = ent5(nc.sbuf_tensor("srow", [128, PD * D], BF16))
            of32 = ent5(nc.sbuf_tensor("of32", [128, PD * D], F32))
            xf32 = ent5(nc.sbuf_tensor("xf32", [128, PD * D], F32))
            gs5 = ent5(nc.sbuf_tensor("gs5", [128, PD * D], BF16))

            p5v, p5o = [], []

            def emit_store(jj):
                b = jj % PD

                @block.sync
                def _(sy, jj=jj, b=b):
                    sy.wait_ge(sV.h, p5v[jj])
                    sOUT.inc(sy.dma_start(out=y_out[jj * 128:(jj + 1) * 128, :],
                                          in_=of32[:, b * D:(b + 1) * D]), 16)
                p5o.append(sOUT.n)

            for jj in (range(32) if STAGE >= 50 else []):
                b = jj % PD

                @block.gpsimd
                def _(g, jj=jj, b=b):
                    g.wait_ge(sCC.h, ag3_done)
                    g.wait_ge(sV.h, p5idx_done)
                    if jj >= PD:
                        g.wait_ge(sV.h, p5v[jj - PD])
                    sG5.inc(g.indirect_dma_start(
                        out=mg[:, b * D:(b + 1) * D], out_offset=None,
                        in_=ag3_out.ap(),
                        in_offset=bass.IndirectOffsetOnAxis(ap=slotT_i[:, jj:jj + 1], axis=0)), 16)

                @block.sync
                def _(sy, jj=jj, b=b):
                    if jj >= PD:
                        sy.wait_ge(sV.h, p5v[jj - PD])
                    sy.wait_ge(sS4.h, s4_marks[jj])
                    sP5.inc(sy.dma_start(out=xrow[:, b * D:(b + 1) * D],
                                         in_=x_own[jj * 128:(jj + 1) * 128, :]), 16)
                    sP5.inc(sy.dma_start(out=xnr[:, b * D:(b + 1) * D],
                                         in_=xn_sc.ap()[jj * 128:(jj + 1) * 128, :]), 16)
                    sP5.inc(sy.dma_start(out=srow[:, b * D:(b + 1) * D],
                                         in_=s_sc.ap()[jj * 128:(jj + 1) * 128, :]), 16)
                if jj >= PD:
                    emit_store(jj - PD)

                @block.vector
                def _(v, jj=jj, b=b):
                    v.wait_ge(sG5.h, 16 * (jj + 1))
                    v.wait_ge(sP5.h, 48 * (jj + 1))
                    if jj >= PD:
                        v.wait_ge(sOUT.h, p5o[jj - PD])
                    v.tensor_scalar(of32[:, b * D:(b + 1) * D], mg[:, b * D:(b + 1) * D],
                                    gmask[:, jj:jj + 1], None, ALU.mult)
                    v.tensor_copy(xf32[:, b * D:(b + 1) * D], xnr[:, b * D:(b + 1) * D])
                    v.tensor_tensor(of32[:, b * D:(b + 1) * D], of32[:, b * D:(b + 1) * D],
                                    xf32[:, b * D:(b + 1) * D], ALU.add)
                    vfin(v, sV)
                ffn_done = sV.n

                @block.scalar
                def _(sc, b=b, ffn_done=ffn_done):
                    sc.wait_ge(sV.h, ffn_done)
                    sc.activation(gs5[:, b * D:(b + 1) * D], of32[:, b * D:(b + 1) * D], AF.Silu)
                    sfin(sc, sSc, op="silu")
                g5_done = sSc.n

                @block.vector
                def _(v, b=b, g5_done=g5_done):
                    v.wait_ge(sSc.h, g5_done)
                    v.tensor_tensor(xf32[:, b * D:(b + 1) * D], gs5[:, b * D:(b + 1) * D],
                                    srow[:, b * D:(b + 1) * D], ALU.mult)
                    v.tensor_scalar(jkv[:, 1:2], jkv[:, 1:2], 1.0, None, ALU.mult)
                    v.tensor_tensor(of32[:, b * D:(b + 1) * D], xf32[:, b * D:(b + 1) * D],
                                    xrow[:, b * D:(b + 1) * D], ALU.add)
                    vfin(v, sV)
                p5v.append(sV.n)

            for jj in (range(32 - PD, 32) if STAGE >= 50 else []):
                emit_store(jj)

            allc = [sOUT, sST3, sV, sSc, sT_, sG, sG3, sCC, sS2, sXNF, sS4,
                    sY3, sTD, sWE, sG5, sP5, sX, sXN, sW, sL3]

            @block.sync
            def _(sy):
                for c in allc:
                    if c.n:
                        sy.wait_ge(c.h, c.n)

            @block.gpsimd
            def _(g):
                for c in allc:
                    if c.n:
                        g.wait_ge(c.h, c.n)

    return nc


_NC_CACHE = {}


def kernel(L0_attn_x, L0_attn_decay, L0_attn_Wq, L0_attn_Wv, L0_attn_Wo, L0_attn_Wg,
           L0_attn_ln_gamma, L0_attn_ln_beta, L0_ffn_w_router, L0_ffn_w_gate,
           L0_ffn_w_up, L0_ffn_w_down, _debug=False):
    x = np.ascontiguousarray(np.asarray(L0_attn_x, dtype=np.float32).reshape(T, D))
    wq_t = np.ascontiguousarray(np.asarray(L0_attn_Wq, np.float32).T)
    wv_t = np.ascontiguousarray(np.asarray(L0_attn_Wv, np.float32).T)
    wg_t = np.ascontiguousarray(np.asarray(L0_attn_Wg, np.float32).T)
    wo_t = np.ascontiguousarray(np.asarray(L0_attn_Wo, np.float32).T)
    wr = np.ascontiguousarray(np.asarray(L0_ffn_w_router, np.float32))
    weg_a = np.asarray(L0_ffn_w_gate, np.float32)
    weu_a = np.asarray(L0_ffn_w_up, np.float32)
    wed_a = np.asarray(L0_ffn_w_down, np.float32)
    gam = np.ascontiguousarray(np.asarray(L0_attn_ln_gamma, np.float32).reshape(NCH, 128).T)
    bet = np.ascontiguousarray(np.asarray(L0_attn_ln_beta, np.float32).reshape(NCH, 128).T)
    lam = np.ascontiguousarray(np.repeat(np.asarray(L0_attn_decay, np.float32), HD).reshape(NCH, 128).T)
    lamb_bt = np.ascontiguousarray(np.repeat(lam, P4T, axis=1))

    iotae = np.tile(np.arange(E, dtype=np.float32), (128, 1))
    c32a = np.tile(np.arange(E, dtype=np.float32) - E, (128, 1))
    identa = np.eye(128, dtype=np.float32)
    ltua = np.triu(np.ones((128, 128), np.float32), 1).astype(ml_dtypes.bfloat16)
    onca = np.ones((128, 1), ml_dtypes.bfloat16)
    onra = np.ones((1, 128), np.float32)

    global _NC_CACHE
    key = (bool(DEBUG), STAGE)
    if key not in _NC_CACHE:
        _NC_CACHE[key] = build_nc()
    nc = _NC_CACHE[key]

    in_maps = []
    for k in range(NCORE):
        p_cc, cc_i = np.meshgrid(np.arange(128), np.arange(32), indexing="ij")
        ids = (NT * k + p_cc * 32 + cc_i).astype(np.float32)
        widxa = (40 * k + np.arange(40, dtype=np.int32)).reshape(40, 1)
        perma = np.zeros((128, 128), np.float32)
        for m in range(16):
            perma[16 * k + m, m] = 1.0
        in_maps.append({
            "x_own": x[NT * k:NT * (k + 1)],
            "x_all": x,
            "wq_t": wq_t, "wv_t": wv_t, "wg_t": wg_t, "wo_t": wo_t,
            "wr_h": wr,
            "weg": np.ascontiguousarray(weg_a[NE * k:NE * (k + 1)]),
            "weu": np.ascontiguousarray(weu_a[NE * k:NE * (k + 1)]),
            "wed": np.ascontiguousarray(wed_a[NE * k:NE * (k + 1)]),
            "gam_h": gam, "bet_h": bet, "lam_h": lam, "lamb_h": lamb_bt,
            "ids_h": ids, "widx_h": widxa, "perm_h": perma,
            "iotae_h": iotae, "c32_h": c32a, "ident_h": identa,
            "ltu_h": ltua, "onc_h": onca, "onr_h": onra,
        })
    res = run_bass_kernel_spmd(nc, in_maps, core_ids=list(range(NCORE)))
    out = np.concatenate([res.results[k]["y_out"] for k in range(NCORE)], axis=0)
    out = out.reshape(B, S, D)
    if _debug:
        return out, res
    return out
